# version 01
# baseline (speedup 1.0000x reference)
import os
import math
import numpy as np
import ml_dtypes
from contextlib import ExitStack

import concourse.bass as bass
import concourse.mybir as mybir
from concourse.bass_utils import run_bass_kernel_spmd

F32 = mybir.dt.float32
BF16 = mybir.dt.bfloat16
AF = mybir.ActivationFunctionType
ALU = mybir.AluOpType
AXX = mybir.AxisListType.X

BATCH, T, D = 4, 4096, 1024
NCORE = 8
TLOC = 2048
NH, HD = 16, 64
HLOC = 8
NCH = 8
DFF = 4096
NEXP = 8
EPS_LN = 1e-5
EPS_RMS = 1e-6
EPS_Q = 1e-5

DEBUG = os.environ.get("MK_DEBUG", "") == "1"
STAGE = int(os.environ.get("MK_STAGE", "99"))

ENGS = ("sync", "scalar", "vector", "tensor", "gpsimd")
DOMS = ENGS + ("dq_sync", "dq_scalar", "dq_gpsimd", "cc")
KLANE = int(os.environ.get("MK_KLANE", "8"))
CCSTEP = 1
DMA_Q = {"sync": "dq_sync", "scalar": "dq_scalar", "gpsimd": "dq_gpsimd"}


class Buf:
    __slots__ = ("name", "last_w", "readers")

    def __init__(self, name):
        self.name = name
        self.last_w = None
        self.readers = []


class KB:

    def __init__(self):
        self.ops = []
        self.cnt = {d: 0 for d in DOMS}

    def _edges(self, reads, writes, me):
        deps = set()
        for b in reads:
            if b.last_w is not None:
                deps.add(b.last_w)
        for b in writes:
            if b.last_w is not None:
                deps.add(b.last_w)
            deps.update(b.readers)
        for b in reads:
            b.readers.append(me)
        for b in writes:
            b.last_w = me
            b.readers = []
        deps.discard(me)
        return deps

    def op(self, eng, fn, reads=(), writes=(), dma=False, cc=False, tag=""):
        dom = "cc" if cc else (DMA_Q[eng] if dma else eng)
        seq = self.cnt[dom]
        self.cnt[dom] += 1
        me = (dom, seq)
        deps = self._edges(tuple(reads), tuple(writes), me)
        fns = fn if isinstance(fn, list) else [fn]
        self.ops.append(dict(eng=eng, dom=dom, seq=seq, fns=fns, deps=deps, tag=tag))
        return me

    def barrier(self):
        b = (Buf("bar"),)
        extra = set()
        for d in ("dq_sync", "dq_scalar", "dq_gpsimd"):
            for j in range(min(KLANE, self.cnt[d])):
                extra.add((d, self.cnt[d] - 1 - j))
        if self.cnt["cc"] > 0:
            extra.add(("cc", self.cnt["cc"] - 1))
        first = True
        for e in ENGS:
            me = self.op(e, lambda eng: eng.nop(), writes=b, tag="barw")
            self.ops[-1]["deps"].update(extra)
        for e in ENGS:
            self.op(e, lambda eng: eng.nop(), reads=b, tag="barr")

    def finalize(self, block, sems):
        needed = {}
        for o in self.ops:
            for (d, s) in o["deps"]:
                assert s < self.cnt[d]
                needed.setdefault(d, set()).add(s)
        carrier_val = {}
        for d, seqs in needed.items():
            if d.startswith("dq_") or d == "cc":
                continue
            val, tot = {}, 0
            for s in sorted(seqs):
                tot += 1
                val[s] = tot
            carrier_val[d] = val
        streams = {e: [] for e in ENGS}
        highwater = {}

        def wait_for(engine, ename, d, s):
            if d.startswith("dq_"):
                lane, v = s % KLANE, 16 * (s // KLANE + 1)
                if v > highwater.get((ename, d, lane), 0):
                    engine.wait_ge(sems[d][lane], v)
                    highwater[(ename, d, lane)] = v
            elif d == "cc":
                v = CCSTEP * (s + 1)
                if v > highwater.get((ename, d), 0):
                    engine.wait_ge(sems[d], v)
                    highwater[(ename, d)] = v
            else:
                v = carrier_val[d][s]
                if v > highwater.get((ename, d), 0):
                    engine.wait_ge(sems[d], v)
                    highwater[(ename, d)] = v

        def mk(o):
            deps = sorted(o["deps"])
            ename, dom, seq, fns = o["eng"], o["dom"], o["seq"], o["fns"]

            def emit(engine):
                for (d, s) in deps:
                    wait_for(engine, ename, d, s)
                if os.environ.get("MK_NOTHROTTLE", "") != "1":
                    if dom.startswith("dq_") and seq >= KLANE:
                        wait_for(engine, ename, dom, seq - KLANE)
                    elif dom == "cc" and seq >= 1:
                        wait_for(engine, ename, dom, seq - 1)
                ins = None
                for f in fns:
                    ins = f(engine)
                if dom.startswith("dq_"):
                    ins.then_inc(sems[dom][seq % KLANE], 16)
                elif dom == "cc":
                    ins.then_inc(sems[dom], CCSTEP)
                elif carrier_val.get(dom, {}).get(seq) is not None:
                    ins.then_inc(sems[dom], 1)

            return emit

        for o in self.ops:
            streams[o["eng"]].append(mk(o))
        for ename in ENGS:
            fl = streams[ename]

            def run_all(engine, fl=fl):
                for f in fl:
                    f(engine)

            getattr(block, ename)(run_all)



def _chunked(v):
    return np.ascontiguousarray(np.asarray(v, np.float32).reshape(NCH, 128).T)


def _host_inputs(inp, core):
    b, par = core // 2, core % 2
    t0 = par * TLOC
    x = np.asarray(inp["L0_attn_x"], np.float32)
    d = {}
    d["x_in"] = np.ascontiguousarray(x[b, t0:t0 + TLOC, :].T)
    for kk in ("L0_attn_wq", "L0_attn_wk", "L0_attn_wv", "L0_attn_wo",
               "L1_attn_wq", "L1_attn_wk", "L1_attn_wv", "L1_attn_wg",
               "L1_attn_wo"):
        d[kk + "_T"] = np.ascontiguousarray(np.asarray(inp[kk], np.float32).T)
    for kk in ("L0_attn_gq", "L0_attn_gk", "L0_attn_gv", "L0_attn_go",
               "L1_attn_bq", "L1_attn_bk", "L1_attn_bv", "L1_attn_bg",
               "L1_attn_bo", "L1_attn_gamma", "L1_attn_beta"):
        d[kk + "_c"] = _chunked(inp[kk])
    d["L0_ffn_WrT"] = np.ascontiguousarray(np.asarray(inp["L0_ffn_Wr"], np.float32).T)
    d["L0_ffn_br"] = np.asarray(inp["L0_ffn_br"], np.float32).reshape(8, 1)
    d["L0_ffn_W1T"] = np.ascontiguousarray(np.asarray(inp["L0_ffn_W1"], np.float32).transpose(0, 2, 1))
    d["L0_ffn_b1_c"] = np.ascontiguousarray(
        np.asarray(inp["L0_ffn_b1"], np.float32).reshape(NEXP, 32, 128).transpose(0, 2, 1))
    d["L0_ffn_W2T"] = np.ascontiguousarray(np.asarray(inp["L0_ffn_W2"], np.float32).transpose(0, 2, 1))
    d["L0_ffn_b2"] = np.asarray(inp["L0_ffn_b2"], np.float32)
    d["L1_attn_wdT"] = np.ascontiguousarray(np.asarray(inp["L1_attn_wd"], np.float32).T)
    d["L1_attn_bd"] = np.asarray(inp["L1_attn_bd"], np.float32).reshape(16, 1)
    d["L1_ffn_gwT"] = np.ascontiguousarray(np.asarray(inp["L1_ffn_gate_w"], np.float32).T)
    d["L1_ffn_gb"] = np.asarray(inp["L1_ffn_gate_b"], np.float32).reshape(8, 1)
    d["L1_ffn_ewT"] = np.ascontiguousarray(np.asarray(inp["L1_ffn_expert_w"], np.float32).transpose(0, 2, 1))
    d["L1_ffn_eb_c"] = np.ascontiguousarray(
        np.asarray(inp["L1_ffn_expert_b"], np.float32).reshape(NEXP, NCH, 128).transpose(0, 2, 1))

    d["ident"] = np.eye(128, dtype=np.float32)
    d["ones_col"] = np.ones((128, 1), np.float32)
    i_ = np.arange(128)
    d["m01_mix"] = (i_[:, None] <= i_[None, :]).astype(np.float32)
    d["m3_mix"] = np.maximum(i_[None, :] - i_[:, None], 0).astype(np.float32)
    d["ramp64"] = np.tile(np.arange(128, dtype=np.float32) - 64.0, 32).reshape(1, 4096)
    d["jrow"] = np.arange(1024, dtype=np.float32).reshape(1, 1024)
    k0q0 = np.zeros((128, 4 * 32), np.float32)
    for qt in range(4):
        for kt in range(32):
            k0q0[:, qt * 32 + kt] = kt * 128 + 64 - qt * 1024
    d["k0q0"] = k0q0
    pos = np.arange(t0, t0 + TLOC, dtype=np.float64)
    dim = np.arange(HD)
    inv_freq = 1.0 / (10000.0 ** ((dim % 32).astype(np.float64) / 32.0))
    ang = pos[None, :] * inv_freq[:, None]
    c_t = np.cos(ang).astype(np.float32)
    s_t = np.sin(ang).astype(np.float32)
    d["ropeC"] = np.ascontiguousarray(np.concatenate([c_t, c_t], axis=0))
    d["ropeS"] = np.ascontiguousarray(np.concatenate([s_t, s_t], axis=0))
    d["ones4096bf"] = np.ones((1, 4096), ml_dtypes.bfloat16)
    sel2 = np.zeros((2, 128), np.float32)
    sel2[0, 0:64] = 1.0
    sel2[1, 64:128] = 1.0
    d["sel2"] = sel2
    d["eidx8"] = np.broadcast_to(np.arange(8, dtype=np.float32), (128, 8)).copy()
    d["sel2T"] = np.ascontiguousarray(sel2.T)
    d["rampbf"] = np.tile(np.arange(128) - 64, 32).reshape(1, 4096).astype(ml_dtypes.bfloat16)
    d["coremeta"] = np.array([[par, 0]], np.int32)
    pt = np.zeros((128, 128), np.float32)
    for blk in (0, 64):
        for i2 in range(0, 64, 2):
            pt[blk + i2 + 1, blk + i2] = -1.0
            pt[blk + i2, blk + i2 + 1] = 1.0
    d["permT"] = pt
    return d



def build_nc():
    nc = bass.Bass("TRN2", target_bir_lowering=False, debug=False)
    es = ExitStack()
    nc._es = es
    A = {}

    def din(name, shape, dtype=F32):
        A[name] = nc.dram_tensor(name, list(shape), dtype, kind="ExternalInput")

    def dout(name, shape, dtype=F32):
        A[name] = nc.dram_tensor(name, list(shape), dtype, kind="ExternalOutput")

    dbg_list = []

    def dint(name, shape, dtype=F32):
        A[name] = nc.dram_tensor(name, list(shape), dtype)
        if DEBUG and (name.startswith("ag") or name in ("x_hbm", "gate_hbm")):
            dbg_list.append((name, list(shape), dtype))

    din("x_in", [D, TLOC])
    for w in ("L0_attn_wq_T", "L0_attn_wk_T", "L0_attn_wv_T", "L0_attn_wo_T",
              "L1_attn_wq_T", "L1_attn_wk_T", "L1_attn_wv_T", "L1_attn_wg_T",
              "L1_attn_wo_T"):
        din(w, [D, D])
    for v in ("L0_attn_gq_c", "L0_attn_gk_c", "L0_attn_gv_c", "L0_attn_go_c",
              "L1_attn_bq_c", "L1_attn_bk_c", "L1_attn_bv_c", "L1_attn_bg_c",
              "L1_attn_bo_c", "L1_attn_gamma_c", "L1_attn_beta_c"):
        din(v, [128, NCH])
    din("L0_ffn_WrT", [D, 8]); din("L0_ffn_br", [8, 1])
    din("L0_ffn_W1T", [NEXP, D, DFF]); din("L0_ffn_b1_c", [NEXP, 128, 32])
    din("L0_ffn_W2T", [NEXP, DFF, D]); din("L0_ffn_b2", [NEXP, D])
    din("L1_attn_wdT", [D, 16]); din("L1_attn_bd", [16, 1])
    din("L1_ffn_gwT", [D, 8]); din("L1_ffn_gb", [8, 1])
    din("L1_ffn_ewT", [NEXP, D, D]); din("L1_ffn_eb_c", [NEXP, 128, 8])
    din("ident", [128, 128]); din("ones_col", [128, 1])
    din("m01_mix", [128, 128]); din("m3_mix", [128, 128])
    din("ramp64", [1, 4096]); din("jrow", [1, 1024]); din("k0q0", [128, 128])
    din("ropeC", [128, TLOC]); din("ropeS", [128, TLOC]); din("permT", [128, 128])
    din("ones4096bf", [1, 4096], BF16); din("rampbf", [1, 4096], BF16)
    din("sel2", [2, 128]); din("eidx8", [128, 8]); din("sel2T", [128, 2])
    din("coremeta", [1, 2], mybir.dt.int32)
    dout("out", [D, TLOC])

    dint("x_hbm", [D, TLOC])
    dint("agq_in", [D, TLOC], BF16); dint("agq_out", [2 * D, TLOC], BF16)
    dint("agk_in", [D, TLOC], BF16); dint("agk_out", [2 * D, TLOC], BF16)
    dint("agv_in", [D, TLOC], BF16); dint("agv_out", [2 * D, TLOC], BF16)
    dint("ago_in", [512, T], BF16); dint("ago_out", [D, T], BF16)
    dint("ags_in", [32, 1024]); dint("ags_out", [64, 1024])
    dint("agst_in", [128, 16]); dint("agst_out", [256, 16])
    dint("agg_in", [1, 32]); dint("agg_out", [2, 32])
    dint("ld_hbm", [16, 1])
    dint("agc_in", [1, 8]); dint("agc_out", [8, 8])
    dint("myq", [D, TLOC], BF16); dint("myk", [D, TLOC], BF16)
    dint("myv", [D, TLOC], BF16)
    dint("onorm_hbm", [D, TLOC], BF16)
    dint("gate_hbm", [D, TLOC], BF16)

    HB = {n: Buf("hb_" + n) for n in A}

    k = KB()
    sems = {}
    for dn in DOMS:
        if dn.startswith("dq_"):
            sems[dn] = [es.enter_context(nc.semaphore("s_%s_%d" % (dn, i)))
                        for i in range(KLANE)]
        else:
            sems[dn] = es.enter_context(nc.semaphore("s_" + dn))

    def sb(name, shape, dtype=F32):
        return es.enter_context(nc.sbuf_tensor(name, list(shape), dtype))

    PS = [es.enter_context(nc.psum_tensor("ps%d" % i, [128, 1024], F32)) for i in range(4)]
    psbuf = [Buf("ps%d_%d" % (i, j)) for i in range(4) for j in range(2)]

    def psb(i, lo=0, hi=1024):
        out = []
        if lo < 512:
            out.append(psbuf[2 * i])
        if hi > 512:
            out.append(psbuf[2 * i + 1])
        return out

    def dma(eng, out_ap, in_ap, reads, writes, tag=""):
        return k.op(eng, lambda e, o=out_ap, i=in_ap: e.dma_start(out=o, in_=i),
                    reads=reads, writes=writes, dma=True, tag=tag)

    def vTT(out_ap, a_ap, b_ap, alu, reads, writes, tag="", eng="vector"):
        k.op(eng, lambda e, o=out_ap, a=a_ap, b=b_ap, al=alu: e.tensor_tensor(
            out=o, in0=a, in1=b, op=al), reads=reads, writes=writes, tag=tag)

    def vTS(out_ap, a_ap, s1, alu, reads, writes, s2=None, alu2=None, tag="", eng="vector"):
        if alu2 is None:
            k.op(eng, lambda e, o=out_ap, a=a_ap, s1=s1, al=alu: e.tensor_scalar(
                out=o, in0=a, scalar1=s1, scalar2=None, op0=al),
                reads=reads, writes=writes, tag=tag)
        else:
            k.op(eng, lambda e, o=out_ap, a=a_ap, s1=s1, s2=s2, al=alu, al2=alu2:
                 e.tensor_scalar(out=o, in0=a, scalar1=s1, scalar2=s2, op0=al, op1=al2),
                 reads=reads, writes=writes, tag=tag)

    def vSTT(out_ap, a_ap, s, b_ap, alu0, alu1, reads, writes, tag="", eng="vector"):
        k.op(eng, lambda e, o=out_ap, a=a_ap, s=s, b=b_ap, a0=alu0, a1=alu1:
             e.scalar_tensor_tensor(out=o, in0=a, scalar=s, in1=b, op0=a0, op1=a1),
             reads=reads, writes=writes, tag=tag)

    def vCP(out_ap, in_ap, reads, writes, tag="", eng="vector"):
        k.op(eng, lambda e, o=out_ap, i=in_ap: e.tensor_copy(out=o, in_=i),
             reads=reads, writes=writes, tag=tag)

    def act(out_ap, in_ap, func, reads, writes, bias=0.0, scale=1.0, accum=None, tag=""):
        P = in_ap.shape[0]
        extra = []
        if isinstance(bias, float) and bias != 0.0:
            bias = fconst(bias, P)
            extra.append(cst_b)
        if isinstance(scale, float) and scale != 1.0:
            scale = fconst(scale, P)
            extra.append(cst_b)
        k.op("scalar", lambda e, o=out_ap, i=in_ap, f=func, b=bias, s=scale, ac=accum:
             e.activation(out=o, in_=i, func=f, bias=b, scale=s, accum_out=ac),
             reads=tuple(reads) + tuple(extra), writes=writes, tag=tag)

    CON = {}
    for cname, shp in (
            ("ident", [128, 128]), ("ones_col", [128, 1]),
            ("m01_mix", [128, 128]), ("m3_mix", [128, 128]),
            ("k0q0", [128, 128]), ("permT", [128, 128]), ("sel2", [2, 128]),
            ("eidx8", [128, 8]), ("sel2T", [128, 2]),
            ("L0_attn_gq_c", [128, NCH]), ("L0_attn_gk_c", [128, NCH]),
            ("L0_attn_gv_c", [128, NCH]), ("L0_attn_go_c", [128, NCH]),
            ("L1_attn_bq_c", [128, NCH]), ("L1_attn_bk_c", [128, NCH]),
            ("L1_attn_bv_c", [128, NCH]), ("L1_attn_bg_c", [128, NCH]),
            ("L1_attn_bo_c", [128, NCH]), ("L1_attn_gamma_c", [128, NCH]),
            ("L1_attn_beta_c", [128, NCH]),
            ("L0_ffn_br", [8, 1]), ("L1_attn_bd", [16, 1]), ("L1_ffn_gb", [8, 1]),
    ):
        t = sb("c_" + cname, shp, F32)
        b = Buf("c_" + cname)
        dma("sync", t[:, :], A[cname].ap()[:, :], reads=(HB[cname],), writes=(b,),
            tag="ldc")
        CON[cname] = (t, b)
    CB = {}
    for cname in ("permT", "m01_mix", "ident", "ones_col"):
        t0c, b0c = CON[cname]
        t = sb("cb_" + cname, list(t0c.shape), BF16)
        b = Buf("cb_" + cname)
        vCP(t[:, :], t0c[:, :], (b0c,), (b,), tag="castc")
        CB[cname] = (t, b)

    abuf_t = sb("abuf", [128, NCH * TLOC], BF16)
    abuf_b = Buf("abuf")
    bbuf_t = sb("bbuf", [128, NCH * TLOC], BF16)
    bbuf_b = Buf("bbuf")
    stg_t = sb("stg", [128, 4096], BF16)
    stg_b = [Buf("stg0"), Buf("stg1")]
    tmpA_t = sb("tmpA", [128, 2048], F32)
    tmpA_b = Buf("tmpA")
    tmpB_t = sb("tmpB", [128, 2048], F32)
    tmpB_b = Buf("tmpB")
    tmpC_t = sb("tmpC", [128, 2048], F32)
    tmpC_b = Buf("tmpC")
    tmpD_t = sb("tmpD", [128, 2048], F32)
    tmpD_b = Buf("tmpD")
    xc_t = sb("xc", [128, 2048], F32)
    xc_b = Buf("xc")
    bc1_t = sb("bc1", [128, 2048], F32)
    bc1_b = Buf("bc1")
    bc2_t = sb("bc2", [128, 2048], F32)
    bc2_b = Buf("bc2")
    BC3 = {}
    rowA_t = sb("rowA", [1, 2048], F32)
    rowA_b = Buf("rowA")
    rowB_t = sb("rowB", [1, 2048], F32)
    rowB_b = Buf("rowB")
    tks_t = sb("tks", [128, 64], F32)
    tks_b = Buf("tks")
    sm_t = sb("sm", [128, 64], F32)
    sm_b = Buf("sm")
    cst_t = sb("cst", [128, 16], F32)
    cst_b = Buf("cst")
    _cst_map = {}

    def fconst(val, P=128):
        if val not in _cst_map:
            j = len(_cst_map)
            assert j < 16
            _cst_map[val] = j
            k.op("vector", lambda e, j=j, v=val: e.memset(cst_t[:, j:j + 1], v),
                 writes=(cst_b,), tag="cst")
        return cst_t[0:P, _cst_map[val]:_cst_map[val] + 1]


    ones_row_f32 = sb("ones_row_f32", [1, 128], F32)
    ones_row_f32_b = Buf("ones_row_f32")
    k.op("vector", lambda e: e.memset(ones_row_f32[:, :], 1.0),
         writes=(ones_row_f32_b,), tag="ones_row")

    onesrow_bf = sb("onesrow_bf", [1, 128], BF16)
    onesrow_bf_b = Buf("onesrow_bf")
    k.op("vector", lambda e: e.memset(onesrow_bf[:, :], 1.0),
         writes=(onesrow_bf_b,), tag="onesrbf")

    meta_t = sb("meta", [1, 2], mybir.dt.int32)
    meta_b = Buf("meta")
    dma("sync", meta_t[:, :], A["coremeta"].ap()[:, :], reads=(HB["coremeta"],),
        writes=(meta_b,), tag="meta")
    REG = {}
    from concourse.expressions import RuntimeValue

    def _load_meta(e):
        rg = e.alloc_register("hhreg")
        e.reg_load(rg, meta_t[0:1, 0:1])
        REG["hh"] = RuntimeValue(rg, 0, 1)
        return e.nop()

    k.op("sync", _load_meta, reads=(meta_b,), tag="metald")

    def bcast_row(dst_t, dst_b, src_t, src_b, tag):
        for s in range(4):
            sl = slice(s * 512, (s + 1) * 512)
            pi = 2 + (s % 2)
            k.op("tensor", lambda e, sl=sl, pi=pi: e.matmul(
                PS[pi][:, 0:512], ones_row_f32[0:1, :],
                src_t[0:1, sl], start=True, stop=True),
                reads=(src_b, ones_row_f32_b), writes=psb(pi, 0, 512),
                tag=tag + "bm")
            vCP(dst_t[:, sl], PS[pi][:, 0:512],
                psb(pi, 0, 512), (dst_b,), tag=tag + "be")

    def round_bf(dst_ap, dst_b, src_ap, src_b, tag):
        vTS(tmpC_t[:, :], src_ap, 128.5, ALU.add, (src_b,), (tmpC_b,), tag=tag + "u0")
        cur_t, cur_b, oth_t, oth_b = tmpC_t, tmpC_b, tmpD_t, tmpD_b
        for b in (128.0, 64.0, 32.0, 16.0, 8.0, 4.0, 2.0, 1.0):
            vTS(tmpB_t[:, :], cur_t[:, :], b, ALU.is_ge, (cur_b,), (tmpB_b,),
                s2=b, alu2=ALU.mult, tag=tag + "g")
            vTT(oth_t[:, :], cur_t[:, :], tmpB_t[:, :], ALU.subtract,
                (cur_b, tmpB_b), (oth_b,), tag=tag + "s")
            cur_t, cur_b, oth_t, oth_b = oth_t, oth_b, cur_t, cur_b
        vTT(tmpB_t[:, :], src_ap, cur_t[:, :], ALU.subtract,
            (src_b, cur_b), (tmpB_b,), tag=tag + "d")
        vTS(dst_ap, tmpB_t[:, :], 0.5, ALU.add, (tmpB_b,), (dst_b,), tag=tag + "f")

    def ln_stats(src, tag, want_comb, also_xbf=False):
        for c in range(NCH):
            dma("sync", xc_t[:, :], A[src].ap()[c * 128:(c + 1) * 128, :],
                reads=(HB[src],), writes=(xc_b,), tag=tag + "ldx")
            vCP(tmpA_t[:, :], xc_t[:, :], (xc_b,), (tmpA_b,), tag=tag + "cp")
            if also_xbf:
                vCP(abuf_t[:, c * TLOC:(c + 1) * TLOC], xc_t[:, :], (xc_b,),
                    (abuf_b,), tag=tag + "xbf")
            vTT(tmpB_t[:, :], tmpA_t[:, :], tmpA_t[:, :], ALU.mult,
                (tmpA_b,), (tmpB_b,), tag=tag + "sq")
            for s in range(4):
                pi = 2 if s < 2 else 3
                lo = (s % 2) * 512
                k.op("tensor", lambda e, c=c, pi=pi, lo=lo, s=s: e.matmul(
                    PS[pi][0:1, lo:lo + 512], CON["ones_col"][0][:, 0:1],
                    tmpA_t[:, s * 512:(s + 1) * 512],
                    start=(c == 0), stop=(c == NCH - 1)),
                    reads=(tmpA_b, CON["ones_col"][1]), writes=psb(pi, lo, lo + 512),
                    tag=tag + "sm")
                pj = 0 if s < 2 else 1
                k.op("tensor", lambda e, c=c, pj=pj, lo=lo, s=s: e.matmul(
                    PS[pj][0:1, lo:lo + 512], CON["ones_col"][0][:, 0:1],
                    tmpB_t[:, s * 512:(s + 1) * 512],
                    start=(c == 0), stop=(c == NCH - 1)),
                    reads=(tmpB_b, CON["ones_col"][1]), writes=psb(pj, lo, lo + 512),
                    tag=tag + "sqm")
        vCP(tmpA_t[0:1, 0:1024], PS[2][0:1, :], psb(2), (tmpA_b,), tag=tag + "e1")
        vCP(tmpA_t[0:1, 1024:2048], PS[3][0:1, :], psb(3), (tmpA_b,), tag=tag + "e2")
        vCP(tmpB_t[0:1, 0:1024], PS[0][0:1, :], psb(0), (tmpB_b,), tag=tag + "e3")
        vCP(tmpB_t[0:1, 1024:2048], PS[1][0:1, :], psb(1), (tmpB_b,), tag=tag + "e4")
        vTS(tmpC_t[0:1, :], tmpA_t[0:1, :], 1.0 / D, ALU.mult, (tmpA_b,), (tmpC_b,), tag=tag + "mn")
        vTT(tmpD_t[0:1, :], tmpC_t[0:1, :], tmpC_t[0:1, :], ALU.mult, (tmpC_b,), (tmpD_b,), tag=tag + "m2")
        vSTT(tmpB_t[0:1, :], tmpB_t[0:1, :], 1.0 / D, tmpD_t[0:1, :],
             ALU.mult, ALU.subtract, (tmpB_b, tmpD_b), (tmpB_b,), tag=tag + "var")
        act(tmpD_t[0:1, :], tmpB_t[0:1, :], AF.Ln, (tmpB_b,), (tmpD_b,), bias=EPS_LN, tag=tag + "ln")
        act(xc_t[0:1, 0:TLOC], tmpD_t[0:1, :], AF.Exp, (tmpD_b,), (xc_b,), scale=-0.5, tag=tag + "rstd")
        if want_comb:
            vTT(tmpD_t[0:1, :], xc_t[0:1, 0:TLOC], xc_t[0:1, 0:TLOC], ALU.mult, (xc_b,), (tmpD_b,), tag=tag + "r2")
            vTT(tmpD_t[0:1, :], tmpD_t[0:1, :], tmpB_t[0:1, :], ALU.mult, (tmpD_b, tmpB_b), (tmpD_b,), tag=tag + "vr2")
            act(tmpD_t[0:1, :], tmpD_t[0:1, :], AF.Ln, (tmpD_b,), (tmpD_b,), bias=EPS_RMS, tag=tag + "ln2")
            act(tmpD_t[0:1, :], tmpD_t[0:1, :], AF.Exp, (tmpD_b,), (tmpD_b,), scale=-0.5, tag=tag + "rs2")
            vTT(rowA_t[0:1, :], xc_t[0:1, 0:TLOC], tmpD_t[0:1, :], ALU.mult, (xc_b, tmpD_b), (rowA_b,), tag=tag + "sc")
        else:
            vCP(rowA_t[0:1, :], xc_t[0:1, 0:TLOC], (xc_b,), (rowA_b,), tag=tag + "rcp")
        vTT(rowB_t[0:1, :], tmpC_t[0:1, :], rowA_t[0:1, :], ALU.mult,
            (tmpC_b, rowA_b), (rowB_b,), tag=tag + "msc")

    WQH = {}

    def weight_quant(wname, dst_ap, dst_b, smcol, tag):
        wq_t, wq_b = WQH["t"], WQH["b"]
        for c in range(NCH):
            dma("sync", wq_t[:, c * 1024:(c + 1) * 1024],
                A[wname].ap()[c * 128:(c + 1) * 128, :],
                reads=(HB[wname],), writes=(wq_b,), tag=tag + "ld")
        k.op("vector", lambda e: e.tensor_reduce(
            out=sm_t[:, 32:33], in_=wq_t[:, :], axis=AXX, op=ALU.add,
            apply_absolute_value=True), reads=(wq_b,), writes=(sm_b,), tag=tag + "as")
        k.op("tensor", lambda e: e.matmul(
            PS[0][0:1, 0:1], sm_t[:, 32:33], CON["ones_col"][0][:, 0:1],
            start=True, stop=True),
            reads=(sm_b, CON["ones_col"][1]), writes=psb(0, 0, 1), tag=tag + "tt")
        vTS(sm_t[0:1, smcol:smcol + 1], PS[0][0:1, 0:1], 1.0 / (1 << 20), ALU.mult,
            psb(0, 0, 1), (sm_b,), tag=tag + "sw")
        k.op("vector", lambda e: e.reciprocal(
            out=sm_t[0:1, 40:41], in_=sm_t[0:1, smcol:smcol + 1]),
            reads=(sm_b,), writes=(sm_b,), tag=tag + "rc")
        k.op("tensor", lambda e: e.matmul(
            PS[0][:, 0:1], ones_row_f32[0:1, :], sm_t[0:1, 40:41],
            start=True, stop=True),
            reads=(sm_b, ones_row_f32_b), writes=psb(0, 0, 1), tag=tag + "rb")
        vCP(sm_t[:, 41:42], PS[0][:, 0:1], psb(0, 0, 1), (sm_b,), tag=tag + "rbe")
        k.op("tensor", lambda e, smcol=smcol: e.matmul(
            PS[0][:, 0:1], ones_row_f32[0:1, :], sm_t[0:1, smcol:smcol + 1],
            start=True, stop=True),
            reads=(sm_b, ones_row_f32_b), writes=psb(0, 0, 1), tag=tag + "sb")
        vCP(sm_t[:, 42:43], PS[0][:, 0:1], psb(0, 0, 1), (sm_b,), tag=tag + "sbe")
        for h in range(4):
            sl = slice(h * 2048, (h + 1) * 2048)
            vTS(tmpA_t[:, :], wq_t[:, sl], sm_t[:, 41:42], ALU.mult,
                (wq_b, sm_b), (tmpA_b,), tag=tag + "v")
            vTS(tmpB_t[:, :], tmpA_t[:, :], 0.5, ALU.is_gt, (tmpA_b,), (tmpB_b,), tag=tag + "gp")
            vTS(tmpD_t[:, :], tmpA_t[:, :], -0.5, ALU.is_lt, (tmpA_b,), (tmpD_b,), tag=tag + "gn")
            vTT(dst_ap[:, sl], tmpB_t[:, :], tmpD_t[:, :], ALU.subtract,
                (tmpB_b, tmpD_b), (dst_b,), tag=tag + "tau")

    def tokmaj_to_row(src_ap, src_b, tag):
        k.op("tensor", lambda e: e.transpose(
            PS[0][0:16, 0:128], src_ap, CON["ident"][0][:, :]),
            reads=(src_b, CON["ident"][1]), writes=psb(0, 0, 128), tag=tag + "tt")
        vCP(tmpD_t[0:16, 0:128], PS[0][0:16, 0:128], psb(0, 0, 128), (tmpD_b,),
            tag=tag + "te")
        dma("sync", xc_t[0:1, 0:TLOC], tmpD_t[0:16, 0:128],
            reads=(tmpD_b,), writes=(xc_b,), tag=tag + "t2r")

    def bitlin_quant(gname, src, tag, src_bf16=False):
        g_t, g_b = CON[gname]
        ldeng = "gpsimd" if src_bf16 else "sync"
        for c in range(NCH):
            dma(ldeng, xc_t[:, :], A[src].ap()[c * 128:(c + 1) * 128, :],
                reads=(HB[src],), writes=(xc_b,), tag=tag + "x1")
            vTT(tmpA_t[:, :], xc_t[:, :], bc1_t[:, 0:TLOC], ALU.mult,
                (xc_b, bc1_b), (tmpA_b,), tag=tag + "t1")
            vSTT(tmpB_t[:, :], bc2_t[:, 0:TLOC], -1.0, tmpA_t[:, :],
                 ALU.mult, ALU.add, (bc2_b, tmpA_b), (tmpB_b,), tag=tag + "t2")
            vTS(tmpA_t[:, :], tmpB_t[:, :], g_t[:, c:c + 1], ALU.mult,
                (tmpB_b, g_b), (tmpA_b,), tag=tag + "a0")
            for tb in range(16):
                pi = tb % 2
                k.op("tensor", lambda e, tb=tb, pi=pi: e.transpose(
                    PS[pi][:, 0:128], tmpA_t[:, tb * 128:(tb + 1) * 128],
                    CON["ident"][0][:, :]),
                    reads=(tmpA_b, CON["ident"][1]), writes=psb(pi, 0, 128),
                    tag=tag + "tr")
                k.op("vector", lambda e, tb=tb, pi=pi: e.tensor_reduce(
                    out=tks_t[:, 16 + tb:16 + tb + 1],
                    in_=PS[pi][:, 0:128], axis=AXX, op=ALU.max,
                    apply_absolute_value=True),
                    reads=psb(pi, 0, 128), writes=(tks_b,), tag=tag + "mx")
                if c == 0:
                    vCP(tks_t[:, tb:tb + 1], tks_t[:, 16 + tb:16 + tb + 1],
                        (tks_b,), (tks_b,), tag=tag + "mi")
                else:
                    vTT(tks_t[:, tb:tb + 1], tks_t[:, tb:tb + 1],
                        tks_t[:, 16 + tb:16 + tb + 1], ALU.max,
                        (tks_b,), (tks_b,), tag=tag + "mu")
        vTS(tks_t[:, 32:48], tks_t[:, 0:16], EPS_Q, ALU.max, (tks_b,), (tks_b,), tag=tag + "cl")
        k.op("vector", lambda e: e.reciprocal(out=tks_t[:, 48:64], in_=tks_t[:, 32:48]),
             reads=(tks_b,), writes=(tks_b,), tag=tag + "rc")
        vTS(tks_t[:, 48:64], tks_t[:, 48:64], 127.0, ALU.mult, (tks_b,), (tks_b,), tag=tag + "qs")
        bc3_t, bc3_b = BC3["t"], BC3["b"]
        tokmaj_to_row(tks_t[:, 48:64], tks_b, tag + "q")
        bcast_row(bc3_t, bc3_b, xc_t, xc_b, tag + "qb")
        for c in range(NCH):
            dma(ldeng, xc_t[:, :], A[src].ap()[c * 128:(c + 1) * 128, :],
                reads=(HB[src],), writes=(xc_b,), tag=tag + "x2")
            vTT(tmpA_t[:, :], xc_t[:, :], bc1_t[:, 0:TLOC], ALU.mult,
                (xc_b, bc1_b), (tmpA_b,), tag=tag + "u1")
            vSTT(tmpB_t[:, :], bc2_t[:, 0:TLOC], -1.0, tmpA_t[:, :],
                 ALU.mult, ALU.add, (bc2_b, tmpA_b), (tmpB_b,), tag=tag + "u2")
            vSTT(tmpA_t[:, :], tmpB_t[:, :], g_t[:, c:c + 1], BC3["t"][:, 0:TLOC],
                 ALU.mult, ALU.mult, (tmpB_b, g_b, BC3["b"]), (tmpA_b,), tag=tag + "u3")
            round_bf(abuf_t[:, c * TLOC:(c + 1) * TLOC], abuf_b, tmpA_t[:, :],
                     tmpA_b, tag + "rnd%d" % c)

    def projection(tau_ap, tau_bb, dst, tag):
        for oc in range(NCH):
            for s in range(4):
                pi = oc % 2
                lo = (s % 2) * 512
                mm = []
                for c in range(NCH):
                    mm.append(lambda e, oc=oc, s=s, c=c, pi=pi, lo=lo: e.matmul(
                        PS[pi][:, lo:lo + 512],
                        tau_ap[:, c * 1024 + oc * 128: c * 1024 + (oc + 1) * 128],
                        abuf_t[:, c * TLOC + s * 512: c * TLOC + (s + 1) * 512],
                        start=(c == 0), stop=(c == NCH - 1)))
                k.op("tensor", mm, reads=(tau_bb, abuf_b), writes=psb(pi, lo, lo + 512),
                     tag=tag + "mm")
                vTT(stg_t[:, (oc % 2) * 2048 + s * 512:(oc % 2) * 2048 + (s + 1) * 512],
                    PS[pi][:, lo:lo + 512], bc2_t[:, s * 512:(s + 1) * 512], ALU.mult,
                    psb(pi, lo, lo + 512) + [bc2_b], (stg_b[oc % 2],), tag=tag + "ev")
            if isinstance(dst, tuple) and dst[0] == "resid":
                _, xsrc, xdst = dst
                dma("sync", xc_t[:, :], A[xsrc].ap()[oc * 128:(oc + 1) * 128, :],
                    reads=(HB[xsrc],), writes=(xc_b,), tag=tag + "rx")
                vTT(tmpD_t[:, :], stg_t[:, (oc % 2) * 2048:(oc % 2) * 2048 + 2048],
                    xc_t[:, :], ALU.add, (stg_b[oc % 2], xc_b), (tmpD_b,), tag=tag + "ra")
                dma("sync", A[xdst].ap()[oc * 128:(oc + 1) * 128, :], tmpD_t[:, :],
                    reads=(tmpD_b,), writes=(HB[xdst],), tag=tag + "rw")
            elif isinstance(dst, tuple):
                _, dt_t, dt_b = dst
                vCP(dt_t[:, oc * TLOC:(oc + 1) * TLOC],
                    stg_t[:, (oc % 2) * 2048:(oc % 2) * 2048 + 2048],
                    (stg_b[oc % 2],), (dt_b,), tag=tag + "cp")
            else:
                dma("sync", A[dst].ap()[oc * 128:(oc + 1) * 128, :],
                    stg_t[:, (oc % 2) * 2048:(oc % 2) * 2048 + 2048],
                    (stg_b[oc % 2],), (HB[dst],), tag=tag + "st")

    def allgather(name, tag):
        i_ap = A[name + "_in"].ap().opt()
        o_ap = A[name + "_out"].ap().opt()
        k.op("gpsimd", lambda e: e.collective_compute(
            "AllGather", ALU.bypass,
            replica_groups=[[0, 1], [2, 3], [4, 5], [6, 7]],
            ins=[i_ap], outs=[o_ap]),
            reads=(HB[name + "_in"],), writes=(HB[name + "_out"],), cc=True, tag=tag)

    p1es = ExitStack()
    tau0_t = p1es.enter_context(nc.sbuf_tensor("tau0", [128, 8192], BF16))
    tau0_b = Buf("tau0")
    WQH["t"] = p1es.enter_context(nc.sbuf_tensor("wqt", [128, 8192], F32))
    WQH["b"] = Buf("wqt")
    BC3["t"] = p1es.enter_context(nc.sbuf_tensor("bc3a", [128, 2048], F32))
    BC3["b"] = Buf("bc3a")

    ln_stats("x_in", "l0ln", want_comb=True)
    bcast_row(bc1_t, bc1_b, rowA_t, rowA_b, "l0sc")
    bcast_row(bc2_t, bc2_b, rowB_t, rowB_b, "l0msc")

    def l0_proj(gname, wname, tau_ap, tau_bb, smcol, dst, tag, oscale=1.0):
        weight_quant(wname, tau_ap, tau_bb, smcol, tag + "w")
        bitlin_quant(gname, "x_in", tag + "a")
        vTS(tks_t[:, 16:32], tks_t[:, 0:16], sm_t[:, 42:43], ALU.mult,
            (tks_b, sm_b), (tks_b,), s2=oscale / 127.0, alu2=ALU.mult, tag=tag + "os")
        tokmaj_to_row(tks_t[:, 16:32], tks_b, tag + "o")
        bcast_row(bc2_t, bc2_b, xc_t, xc_b, tag + "osb")
        projection(tau_ap, tau_bb, dst, tag + "p")
        bcast_row(bc2_t, bc2_b, rowB_t, rowB_b, tag + "re")

    SKIP_P1 = os.environ.get("MK_SKIP_P1", "") == "1"
    if not SKIP_P1:
        l0_proj("L0_attn_gq_c", "L0_attn_wq_T", tau0_t[:, :], tau0_b, 0, "agq_in", "pq",
                oscale=0.125)
    if not SKIP_P1:
        allgather("agq", "AGq")
        l0_proj("L0_attn_gk_c", "L0_attn_wk_T", tau0_t[:, :], tau0_b, 1, "agk_in", "pk")
        allgather("agk", "AGk")
        l0_proj("L0_attn_gv_c", "L0_attn_wv_T", tau0_t[:, :], tau0_b, 2, "agv_in", "pv")
        allgather("agv", "AGv")
    k.barrier()
    p1es.close()

    def attn_phase(layer, tag):
        pes = ExitStack()
        kst_t = pes.enter_context(nc.sbuf_tensor(tag + "kst", [66, 4096], BF16))
        kst_b = Buf(tag + "kst")
        qst_t = pes.enter_context(nc.sbuf_tensor(tag + "qst", [66, 4096], BF16))
        qst_b = Buf(tag + "qst")
        vst_t = pes.enter_context(nc.sbuf_tensor(tag + "vst", [128, 32 * 65], BF16))
        vst_b = Buf(tag + "vst")
        vh_t = pes.enter_context(nc.sbuf_tensor(tag + "vh", [64, 4096], BF16))
        vh_b = Buf(tag + "vh")
        pt_t = [pes.enter_context(nc.sbuf_tensor(tag + "pt%d" % i, [128, 1024], BF16))
                for i in range(2)]
        pt_b = [Buf(tag + "pt0"), Buf(tag + "pt1")]
        if layer == 1:
            exq_t = pes.enter_context(nc.sbuf_tensor(tag + "exq", [128, 1024], BF16))
            exq_b = Buf(tag + "exq")
            cbs_t = pes.enter_context(nc.sbuf_tensor(tag + "cbs", [128, 32], F32))
            cbs_b = Buf(tag + "cbs")
            ldr_t = pes.enter_context(nc.sbuf_tensor(tag + "ldr", [1, 64], F32))
            ldr_b = Buf(tag + "ldr")
            wdc_t = pes.enter_context(nc.sbuf_tensor(tag + "wdc", [128, 128], F32))
            wdc_b = Buf(tag + "wdc")
            jr_t = pes.enter_context(nc.sbuf_tensor(tag + "jr", [1, 1024], F32))
            jr_b = Buf(tag + "jr")
            dma("sync", jr_t[:, :], A["jrow"].ap()[:, :], reads=(HB["jrow"],),
                writes=(jr_b,), tag=tag + "jrl")

        dma("sync", kst_t[64:65, :], A["ones4096bf"].ap()[:, :],
            reads=(HB["ones4096bf"],), writes=(kst_b,), tag=tag + "k64")
        dma("sync", kst_t[65:66, :], A["rampbf"].ap()[:, :],
            reads=(HB["rampbf"],), writes=(kst_b,), tag=tag + "k65")
        for kt in range(32):
            k.op("vector", lambda e, kt=kt: e.memset(vst_t[:, kt * 65 + 64:kt * 65 + 65], 1.0),
                 writes=(vst_b,), tag=tag + "vone")

        if layer == 1:
            for half in range(2):
                dma("sync", (tmpA_t if half == 0 else tmpB_t)[:, 0:16],
                    A["agst_out"].ap()[half * 128:(half + 1) * 128, :],
                    reads=(HB["agst_out"],), writes=(tmpA_b if half == 0 else tmpB_b,),
                    tag=tag + "ldst")
            vTT(tmpC_t[:, 0:16], tmpA_t[:, 0:16], tmpB_t[:, 0:16], ALU.add,
                (tmpA_b, tmpB_b), (tmpC_b,), tag=tag + "xm1")
            vTS(tmpC_t[:, 0:16], tmpC_t[:, 0:16], 1.0 / T, ALU.mult,
                (tmpC_b,), (tmpC_b,), tag=tag + "xm2")
            for c in range(NCH):
                dma("sync", wdc_t[:, c * 16:(c + 1) * 16],
                    A["L1_attn_wdT"].ap()[c * 128:(c + 1) * 128, :],
                    reads=(HB["L1_attn_wdT"],), writes=(wdc_b,), tag=tag + "wd")
            mmz = []
            for c in range(NCH):
                mmz.append(lambda e, c=c: e.matmul(
                    PS[3][0:16, 0:1], wdc_t[:, c * 16:(c + 1) * 16],
                    tmpC_t[:, c:c + 1], start=(c == 0), stop=(c == NCH - 1)))
            k.op("tensor", mmz, reads=(wdc_b, tmpC_b), writes=psb(3, 0, 1), tag=tag + "zmm")
            vTT(tmpA_t[0:16, 0:1], PS[3][0:16, 0:1], CON["L1_attn_bd"][0][:, 0:1],
                ALU.add, psb(3, 0, 1) + [CON["L1_attn_bd"][1]], (tmpA_b,), tag=tag + "zb")
            act(tmpB_t[0:16, 0:1], tmpA_t[0:16, 0:1], AF.Exp, (tmpA_b,), (tmpB_b,),
                scale=-1.0, tag=tag + "sg1")
            vTS(tmpB_t[0:16, 0:1], tmpB_t[0:16, 0:1], 1.0, ALU.add, (tmpB_b,),
                (tmpB_b,), tag=tag + "sg2")
            k.op("vector", lambda e: e.reciprocal(out=tmpA_t[0:16, 0:1], in_=tmpB_t[0:16, 0:1]),
                 reads=(tmpB_b,), writes=(tmpA_b,), tag=tag + "sg3")
            act(tmpB_t[0:16, 0:1], tmpA_t[0:16, 0:1], AF.Exp, (tmpA_b,), (tmpB_b,),
                scale=-8.0 * math.log(2.0), bias=-5.0 * math.log(2.0), tag=tag + "t1")
            vTS(tmpB_t[0:16, 0:1], tmpB_t[0:16, 0:1], -1.0, ALU.mult, (tmpB_b,),
                (tmpB_b,), s2=1.0, alu2=ALU.add, tag=tag + "t2")
            act(tmpA_t[0:16, 0:1], tmpB_t[0:16, 0:1], AF.Ln, (tmpB_b,), (tmpA_b,),
                tag=tag + "ld")
            dma("sync", A["ld_hbm"].ap()[:, :], tmpA_t[0:16, 0:1],
                reads=(tmpA_b,), writes=(HB["ld_hbm"],), tag=tag + "ldw")
            k.op("sync", lambda e: e.dma_start(
                out=tmpB_t[0:8, 1:2],
                in_=A["ld_hbm"].ap()[bass.ds(REG["hh"] * 8, 8), :]),
                reads=(HB["ld_hbm"],), writes=(tmpB_b,), dma=True, tag=tag + "ldd")
            k.op("tensor", lambda e: e.transpose(
                PS[3][0:1, 0:8], tmpB_t[0:8, 1:2], CON["ident"][0][0:8, 0:8]),
                reads=(tmpB_b, CON["ident"][1]), writes=psb(3, 0, 16), tag=tag + "ldt")
            vCP(ldr_t[0:1, 0:8], PS[3][0:1, 0:8], psb(3, 0, 16), (ldr_b,), tag=tag + "lde")
            vTS(ldr_t[0:1, 16:24], ldr_t[0:1, 0:8], -1.0, ALU.mult, (ldr_b,),
                (ldr_b,), tag=tag + "ldn")


        def bchunks(lo, hi):
            out = []
            c = lo
            while c < hi:
                e2 = min(hi, (c // 512 + 1) * 512)
                out.append((c, e2 - c))
                c = e2
            return out

        for nm, my in (("agk_out", "myk"), ("agq_out", "myq"), ("agv_out", "myv")):
            for half in range(2):
                k.op("sync", lambda e, nm=nm, my=my, half=half: e.dma_start(
                    out=A[my].ap()[half * 512:(half + 1) * 512, :],
                    in_=A[nm].ap()[bass.ds(REG["hh"] * 512 + half * 1024, 512), :]),
                    reads=(HB[nm],), writes=(HB[my],), dma=True, tag=tag + "blk")

        for lh in range(int(os.environ.get('MK_HL', HLOC))):
            for half in range(2):
                csl = slice(half * 2048, (half + 1) * 2048)
                rsl = slice(half * 512 + lh * 64, half * 512 + (lh + 1) * 64)
                dma("sync", kst_t[0:64, csl], A["myk"].ap()[rsl, :],
                    reads=(HB["myk"],), writes=(kst_b,), tag=tag + "ak")
                dma("sync", qst_t[0:64, csl], A["myq"].ap()[rsl, :],
                    reads=(HB["myq"],), writes=(qst_b,), tag=tag + "aq")
                dma("sync", vh_t[0:64, csl], A["myv"].ap()[rsl, :],
                    reads=(HB["myv"],), writes=(vh_b,), tag=tag + "av")
            for kt in range(32):
                k.op("tensor", lambda e, kt=kt: e.transpose(
                    PS[3][:, 0:32].bitcast(BF16), vh_t[0:64, kt * 128:(kt + 1) * 128],
                    CB["ident"][0][0:64, 0:64]),
                    reads=(vh_b, CB["ident"][1]), writes=psb(3, 0, 32), tag=tag + "vt")
                vCP(vst_t[:, kt * 65:kt * 65 + 64], PS[3][:, 0:32].bitcast(BF16),
                    psb(3, 0, 32), (vst_b,), tag=tag + "vte")
            for half in range(2):
                csl = slice(half * 2048, (half + 1) * 2048)
                vTT(stg_t[0:64, 0:2048], qst_t[0:64, csl], qst_t[0:64, csl], ALU.mult,
                    (qst_b,), (stg_b[0], stg_b[1]), tag=tag + "qsq")
                for ss in range(2):
                    mmq = []
                    for s2 in range(2):
                        s = ss * 2 + s2
                        mmq.append(lambda e, s=s, s2=s2: e.matmul(
                            PS[3][0:1, s2 * 512:(s2 + 1) * 512],
                            CB["ones_col"][0][0:64, 0:1],
                            stg_t[0:64, s * 512:(s + 1) * 512], start=True, stop=True))
                    k.op("tensor", mmq, reads=(stg_b[0], stg_b[1], CB["ones_col"][1]),
                         writes=psb(3, 0, 1024), tag=tag + "qn")
                    vCP((tmpA_t if half == 0 else tmpB_t)[0:1, ss * 1024:(ss + 1) * 1024],
                        PS[3][0:1, 0:1024], psb(3, 0, 1024),
                        (tmpA_b if half == 0 else tmpB_b,), tag=tag + "qne")
            for half in range(2):
                csl = slice(half * 2048, (half + 1) * 2048)
                vTT(stg_t[0:64, 0:2048], kst_t[0:64, csl], kst_t[0:64, csl], ALU.mult,
                    (kst_b,), (stg_b[0], stg_b[1]), tag=tag + "ksq")
                for ss in range(2):
                    mmq = []
                    for s2 in range(2):
                        s = ss * 2 + s2
                        mmq.append(lambda e, s=s, s2=s2: e.matmul(
                            PS[3][0:1, s2 * 512:(s2 + 1) * 512],
                            CB["ones_col"][0][0:64, 0:1],
                            stg_t[0:64, s * 512:(s + 1) * 512], start=True, stop=True))
                    k.op("tensor", mmq, reads=(stg_b[0], stg_b[1], CB["ones_col"][1]),
                         writes=psb(3, 0, 1024), tag=tag + "kn")
                    k.op("vector", lambda e, half=half, ss=ss: e.tensor_reduce(
                        out=tmpC_t[0:1, 2 + 2 * half + ss:3 + 2 * half + ss],
                        in_=PS[3][0:1, 0:1024], axis=AXX, op=ALU.max),
                        reads=psb(3, 0, 1024), writes=(tmpC_b,), tag=tag + "knm")
            vTT(tmpC_t[0:1, 2:3], tmpC_t[0:1, 2:3], tmpC_t[0:1, 3:4], ALU.max,
                (tmpC_b,), (tmpC_b,), tag=tag + "km1")
            vTT(tmpC_t[0:1, 4:5], tmpC_t[0:1, 4:5], tmpC_t[0:1, 5:6], ALU.max,
                (tmpC_b,), (tmpC_b,), tag=tag + "km2")
            vTT(tmpC_t[0:1, 0:1], tmpC_t[0:1, 2:3], tmpC_t[0:1, 4:5], ALU.max,
                (tmpC_b,), (tmpC_b,), tag=tag + "km3")
            for half in range(2):
                ht = tmpA_t if half == 0 else tmpB_t
                hb = tmpA_b if half == 0 else tmpB_b
                vTS(ht[0:1, 0:2048], ht[0:1, 0:2048], tmpC_t[0:1, 0:1], ALU.mult,
                    (hb, tmpC_b), (hb,), tag=tag + "bm")
                act(tmpD_t[0:1, 0:2048], ht[0:1, 0:2048], AF.Ln, (hb,), (tmpD_b,),
                    bias=1e-30, tag=tag + "bl")
                act(tmpD_t[0:1, 0:2048], tmpD_t[0:1, 0:2048], AF.Exp, (tmpD_b,),
                    (tmpD_b,), scale=0.5, tag=tag + "be")
                vTS(tmpD_t[0:1, 0:2048], tmpD_t[0:1, 0:2048], -1.0, ALU.mult,
                    (tmpD_b,), (tmpD_b,), tag=tag + "bn")
                vCP(qst_t[64:65, half * 2048:(half + 1) * 2048], tmpD_t[0:1, 0:2048],
                    (tmpD_b,), (qst_b,), tag=tag + "bw")
            if layer == 1:
                vTS(stg_t[0:1, 0:4096], kst_t[64:65, :], ldr_t[0:1, lh:lh + 1],
                    ALU.mult, (kst_b, ldr_b), (stg_b[0],), tag=tag + "ldrow")
                dma("sync", qst_t[65:66, :], stg_t[0:1, 0:4096],
                    reads=(stg_b[0],), writes=(qst_b,), tag=tag + "ld65")
                act(tmpD_t[0:1, 0:1024], jr_t[0:1, :], AF.Exp,
                    (jr_b, ldr_b), (tmpD_b,),
                    scale=ldr_t[0:1, 16 + lh:17 + lh], tag=tag + "eq")
                for s2 in range(2):
                    k.op("tensor", lambda e, s2=s2: e.matmul(
                        PS[3][:, s2 * 512:(s2 + 1) * 512], ones_row_f32[0:1, :],
                        tmpD_t[0:1, s2 * 512:(s2 + 1) * 512], start=True, stop=True),
                        reads=(tmpD_b, ones_row_f32_b), writes=psb(3, s2 * 512, s2 * 512 + 512),
                        tag=tag + "eqb")
                    vCP(exq_t[:, s2 * 512:(s2 + 1) * 512], PS[3][:, s2 * 512:(s2 + 1) * 512],
                        psb(3, s2 * 512, s2 * 512 + 512), (exq_b,), tag=tag + "eqe")
                k.op("tensor", lambda e, lh=lh: e.matmul(
                    PS[3][:, 0:1], ones_row_f32[0:1, :], ldr_t[0:1, lh:lh + 1],
                    start=True, stop=True),
                    reads=(ldr_b, ones_row_f32_b), writes=psb(3, 0, 1), tag=tag + "ldbm")
                vCP(sm_t[:, 44:45], PS[3][:, 0:1], psb(3, 0, 1), (sm_b,), tag=tag + "ldbe")

            for qt in range(4):
                Q0 = qt * 1024
                if layer == 1:
                    vTS(cbs_t[:, 0:32], CON["k0q0"][0][:, qt * 32:(qt + 1) * 32],
                        sm_t[:, 44:45], ALU.mult, (CON["k0q0"][1], sm_b), (cbs_b,),
                        tag=tag + "cb")
                _ktcap = int(os.environ.get('MK_KT', '99'))
                kts = range(0, min(32, 8 * qt + 8, _ktcap)) if layer == 0 else range(min(32, _ktcap))
                first_av = True
                for kt in kts:
                    K0 = kt * 128
                    spi = kt % 2
                    ptb = pt_b[kt % 2]
                    ptt = pt_t[kt % 2]
                    if layer == 0:
                        V0 = max(K0 - Q0, 0)
                        mmg = []
                        for (c0, w) in bchunks(V0, 1024):
                            mmg.append(lambda e, kt=kt, c0=c0, w=w, spi=spi, Q0=Q0:
                                       e.matmul(PS[spi][:, c0:c0 + w],
                                                kst_t[0:65, kt * 128:(kt + 1) * 128],
                                                qst_t[0:65, Q0 + c0:Q0 + c0 + w],
                                                start=True, stop=True))
                        k.op("tensor", mmg, reads=(kst_b, qst_b),
                             writes=psb(spi, V0, 1024), tag=tag + "qk")
                        act(ptt[:, V0:1024], PS[spi][:, V0:1024], AF.Exp,
                            psb(spi, V0, 1024), (ptb,), tag=tag + "ex")
                        if kt >= 8 * qt:
                            vTT(ptt[:, V0:V0 + 128], ptt[:, V0:V0 + 128],
                                CB["m01_mix"][0][:, :], ALU.mult,
                                (ptb, CB["m01_mix"][1]), (ptb,), tag=tag + "msk")
                        avlo = V0
                    else:
                        W = max(0, min(K0 - Q0 + 128, 1024))
                        mmg = []
                        for (c0, w) in bchunks(0, W):
                            mmg.append(lambda e, kt=kt, c0=c0, w=w, spi=spi, Q0=Q0:
                                       e.matmul(PS[spi][:, c0:c0 + w],
                                                kst_t[0:66, kt * 128:(kt + 1) * 128],
                                                qst_t[0:66, Q0 + c0:Q0 + c0 + w],
                                                start=True, stop=True))
                        for (c0, w) in bchunks(W, 1024):
                            mmg.append(lambda e, kt=kt, c0=c0, w=w, spi=spi, Q0=Q0:
                                       e.matmul(PS[spi][:, c0:c0 + w],
                                                kst_t[0:65, kt * 128:(kt + 1) * 128],
                                                qst_t[0:65, Q0 + c0:Q0 + c0 + w],
                                                start=True, stop=True))
                        k.op("tensor", mmg, reads=(kst_b, qst_b),
                             writes=psb(spi, 0, 1024), tag=tag + "qk")
                        mlo = max(K0 - Q0, 0)
                        if mlo < W and K0 - Q0 + 128 > 0 and K0 - Q0 < 1024:
                            w = W - mlo
                            j0 = max(Q0 - K0, 0)
                            k.op("vector", lambda e, spi=spi, mlo=mlo, w=w, j0=j0:
                                 e.scalar_tensor_tensor(
                                     out=PS[spi][:, mlo:mlo + w],
                                     in0=CON["m3_mix"][0][:, j0:j0 + w],
                                     scalar=sm_t[:, 44:45],
                                     in1=PS[spi][:, mlo:mlo + w],
                                     op0=ALU.mult, op1=ALU.add),
                                 reads=(CON["m3_mix"][1], sm_b) + tuple(psb(spi, mlo, mlo + w)),
                                 writes=psb(spi, mlo, mlo + w), tag=tag + "m3")
                        if W > 0:
                            act(ptt[:, 0:W], PS[spi][:, 0:W], AF.Exp,
                                tuple(psb(spi, 0, W)) + (cbs_b,), (ptb,),
                                bias=cbs_t[:, kt:kt + 1], tag=tag + "ex1")
                            vTT(ptt[:, 0:W], ptt[:, 0:W], exq_t[:, 0:W], ALU.mult,
                                (ptb, exq_b), (ptb,), tag=tag + "eqm")
                        if W < 1024:
                            act(ptt[:, W:1024], PS[spi][:, W:1024], AF.Exp,
                                psb(spi, 0, 1024), (ptb,), tag=tag + "ex2")
                        avlo = 0
                    mav = []
                    for (c0, w) in bchunks(avlo, 1024):
                        mav.append(lambda e, kt=kt, c0=c0, w=w, ptt=ptt, fa=first_av:
                                   e.matmul(
                            PS[2][0:65, c0:c0 + w], vst_t[:, kt * 65:(kt + 1) * 65],
                            ptt[:, c0:c0 + w], start=fa, stop=False))
                    k.op("tensor", mav, reads=(vst_b, ptb), writes=psb(2, avlo, 1024),
                         tag=tag + "av")
                    first_av = False
                orow = 0 if lh % 2 == 0 else 64
                oasl = (lh // 2) * 4096 + Q0
                vCP(abuf_t[orow:orow + 64, oasl:oasl + 1024], PS[2][0:64, 0:1024],
                    psb(2), (abuf_b,), tag=tag + "oev")
                vCP(tmpD_t[0:1, 0:1024], PS[2][64:65, 0:1024], psb(2), (tmpD_b,),
                    tag=tag + "sev")
                dma("sync", A["ags_in"].ap()[qt * 8 + lh:qt * 8 + lh + 1, :],
                    tmpD_t[0:1, 0:1024], reads=(tmpD_b,), writes=(HB["ags_in"],),
                    tag=tag + "sst")
        for p in range(4):
            dma("sync", A["ago_in"].ap()[p * 128:(p + 1) * 128, :],
                abuf_t[:, p * 4096:(p + 1) * 4096],
                reads=(abuf_b,), writes=(HB["ago_in"],), tag=tag + "ost")
        allgather("ago", tag + "AGo")
        allgather("ags", tag + "AGs")
        k.barrier()
        pes.close()

    if STAGE >= 2:
        attn_phase(0, "a0")


    def onorm_phase(tag, with_sq_stats):
        pes = ExitStack()
        srec_t = pes.enter_context(nc.sbuf_tensor(tag + "srec", [32, 1024], F32))
        srec_b = Buf(tag + "srec")
        sp_t = pes.enter_context(nc.sbuf_tensor(tag + "sp", [2, 1024], F32))
        sp_b = Buf(tag + "sp")
        recB_t = pes.enter_context(nc.sbuf_tensor(tag + "recB", [128, 2048], F32))
        recB_b = Buf(tag + "recB")
        ost_t = pes.enter_context(nc.sbuf_tensor(tag + "ost", [128, 2048], BF16))
        ost_b = Buf(tag + "ost")
        for cb in range(2):
            k.op("sync", lambda e, cb=cb: e.dma_start(
                out=srec_t[cb * 16:(cb + 1) * 16, :],
                in_=A["ags_out"].ap()[bass.ds(REG["hh"] * 16 + cb * 32, 16), :]),
                reads=(HB["ags_out"],), writes=(srec_b,), dma=True, tag=tag + "sld")
        k.op("vector", lambda e: e.reciprocal(out=srec_t[:, :], in_=srec_t[:, :]),
             reads=(srec_b,), writes=(srec_b,), tag=tag + "rcp")
        for c in range(NCH):
            cb, lh2 = (1, (2 * c) % 8) if c >= 4 else (0, 2 * c)
            for qtp in range(2):
                dma("sync", sp_t[:, :],
                    srec_t[cb * 16 + qtp * 8 + lh2: cb * 16 + qtp * 8 + lh2 + 2, :],
                    reads=(srec_b,), writes=(sp_b,), tag=tag + "spd")
                for s2 in range(2):
                    k.op("tensor", lambda e, qtp=qtp, s2=s2: e.matmul(
                        PS[3][:, s2 * 512:(s2 + 1) * 512], CON["sel2"][0][:, :],
                        sp_t[:, s2 * 512:(s2 + 1) * 512], start=True, stop=True),
                        reads=(sp_b, CON["sel2"][1]), writes=psb(3, s2 * 512, s2 * 512 + 512),
                        tag=tag + "rbm")
                    vCP(recB_t[:, qtp * 1024 + s2 * 512:qtp * 1024 + (s2 + 1) * 512],
                        PS[3][:, s2 * 512:(s2 + 1) * 512],
                        psb(3, s2 * 512, s2 * 512 + 512), (recB_b,), tag=tag + "rbe")
            k.op("sync", lambda e, c=c: e.dma_start(
                out=ost_t[:, :],
                in_=A["ago_out"].ap()[c * 128:(c + 1) * 128, bass.ds(REG["hh"] * 2048, 2048)]),
                reads=(HB["ago_out"],), writes=(ost_b,), dma=True, tag=tag + "old")
            vTT(ost_t[:, :], ost_t[:, :], recB_t[:, :], ALU.mult,
                (ost_b, recB_b), (ost_b,), tag=tag + "nrm")
            dma("sync", A["onorm_hbm"].ap()[c * 128:(c + 1) * 128, :], ost_t[:, :],
                reads=(ost_b,), writes=(HB["onorm_hbm"],), tag=tag + "ow")
            if with_sq_stats:
                vTT(tmpB_t[:, :], ost_t[:, :], ost_t[:, :], ALU.mult,
                    (ost_b,), (tmpB_b,), tag=tag + "sq")
                for s in range(4):
                    pj = 0 if s < 2 else 1
                    lo = (s % 2) * 512
                    k.op("tensor", lambda e, c=c, pj=pj, lo=lo, s=s: e.matmul(
                        PS[pj][0:1, lo:lo + 512], CON["ones_col"][0][:, 0:1],
                        tmpB_t[:, s * 512:(s + 1) * 512],
                        start=(c == 0), stop=(c == NCH - 1)),
                        reads=(tmpB_b, CON["ones_col"][1]), writes=psb(pj, lo, lo + 512),
                        tag=tag + "sqm")
        if with_sq_stats:
            vCP(tmpB_t[0:1, 0:1024], PS[0][0:1, :], psb(0), (tmpB_b,), tag=tag + "e3")
            vCP(tmpB_t[0:1, 1024:2048], PS[1][0:1, :], psb(1), (tmpB_b,), tag=tag + "e4")
            vTS(tmpB_t[0:1, :], tmpB_t[0:1, :], 1.0 / D, ALU.mult, (tmpB_b,), (tmpB_b,),
                tag=tag + "v")
            act(tmpD_t[0:1, :], tmpB_t[0:1, :], AF.Ln, (tmpB_b,), (tmpD_b,),
                bias=EPS_RMS, tag=tag + "l")
            act(rowA_t[0:1, :], tmpD_t[0:1, :], AF.Exp, (tmpD_b,), (rowA_b,),
                scale=-0.5, tag=tag + "e")
            k.op("vector", lambda e: e.memset(rowB_t[0:1, :], 0.0),
                 writes=(rowB_b,), tag=tag + "z")
        k.barrier()
        pes.close()

    if STAGE >= 3:
        onorm_phase("o0", with_sq_stats=True)
        p3es = ExitStack()
        tau3_t = p3es.enter_context(nc.sbuf_tensor("tau3", [128, 8192], BF16))
        tau3_b = Buf("tau3")
        WQH["t"] = p3es.enter_context(nc.sbuf_tensor("wqt3", [128, 8192], F32))
        WQH["b"] = Buf("wqt3")
        BC3["t"] = p3es.enter_context(nc.sbuf_tensor("bc3b", [128, 2048], F32))
        BC3["b"] = Buf("bc3b")
        bcast_row(bc1_t, bc1_b, rowA_t, rowA_b, "o0sc")
        bcast_row(bc2_t, bc2_b, rowB_t, rowB_b, "o0msc")
        weight_quant("L0_attn_wo_T", tau3_t[:, :], tau3_b, 3, "wo")
        bitlin_quant("L0_attn_go_c", "onorm_hbm", "po", src_bf16=True)
        vTS(tks_t[:, 16:32], tks_t[:, 0:16], sm_t[:, 42:43], ALU.mult,
            (tks_b, sm_b), (tks_b,), s2=1.0 / 127.0, alu2=ALU.mult, tag="poos")
        tokmaj_to_row(tks_t[:, 16:32], tks_b, "poo")
        bcast_row(bc2_t, bc2_b, xc_t, xc_b, "poosb")
        projection(tau3_t[:, :], tau3_b, ("resid", "x_in", "x_hbm"), "pop")
        k.barrier()
        p3es.close()


    def xf_build(src, tag):
        for c in range(NCH):
            dma("sync", xc_t[:, :], A[src].ap()[c * 128:(c + 1) * 128, :],
                reads=(HB[src],), writes=(xc_b,), tag=tag + "x")
            vTT(tmpA_t[:, :], xc_t[:, :], bc1_t[:, 0:TLOC], ALU.mult,
                (xc_b, bc1_b), (tmpA_b,), tag=tag + "t1")
            vSTT(abuf_t[:, c * TLOC:(c + 1) * TLOC], bc2_t[:, 0:TLOC], -1.0,
                 tmpA_t[:, :], ALU.mult, ALU.add, (bc2_b, tmpA_b), (abuf_b,),
                 tag=tag + "t2")

    if STAGE >= 4:
        ln_stats("x_hbm", "m0ln", want_comb=False)
        bcast_row(bc1_t, bc1_b, rowA_t, rowA_b, "m0a")
        bcast_row(bc2_t, bc2_b, rowB_t, rowB_b, "m0b")
        xf_build("x_hbm", "m0xf")

        p4es = ExitStack()
        w1q_t = p4es.enter_context(nc.sbuf_tensor("w1q", [128, 8192], BF16))
        w1q_b = Buf("w1q")
        w2q_t = p4es.enter_context(nc.sbuf_tensor("w2q", [128, 8192], BF16))
        w2q_b = Buf("w2q")
        hq_t = p4es.enter_context(nc.sbuf_tensor("hq", [128, 4096], BF16))
        hq_b = Buf("hq")
        mskB_t = p4es.enter_context(nc.sbuf_tensor("mskB", [128, 2048], BF16))
        mskB_b = Buf("mskB")
        lgtm_t = p4es.enter_context(nc.sbuf_tensor("lgtm", [128, 128], F32))
        lgtm_b = Buf("lgtm")
        mskf_t = p4es.enter_context(nc.sbuf_tensor("mskf", [8, 2048], BF16))
        mskf_b = Buf("mskf")
        wr_t = p4es.enter_context(nc.sbuf_tensor("wrq", [128, 16], BF16))
        wr_b = Buf("wrq")
        b1e_t = p4es.enter_context(nc.sbuf_tensor("b1e", [128, 32], F32))
        b1e_b = Buf("b1e")
        b2e_t = p4es.enter_context(nc.sbuf_tensor("b2e", [8, 1024], BF16))
        b2e_b = Buf("b2e")

        for c in range(NCH):
            dma("sync", b1e_t[:, 0:8], A["L0_ffn_WrT"].ap()[c * 128:(c + 1) * 128, :],
                reads=(HB["L0_ffn_WrT"],), writes=(b1e_b,), tag="rwl")
            vCP(wr_t[:, 0:8], b1e_t[:, 0:8], (b1e_b,), (wr_b,), tag="rwc")
            for s in range(4):
                pi = 2 if s < 2 else 3
                lo = (s % 2) * 512
                k.op("tensor", lambda e, c=c, s=s, pi=pi, lo=lo: e.matmul(
                    PS[pi][0:8, lo:lo + 512], wr_t[:, 0:8],
                    abuf_t[:, c * TLOC + s * 512:c * TLOC + (s + 1) * 512],
                    start=(c == 0), stop=(c == NCH - 1)),
                    reads=(wr_b, abuf_b), writes=psb(pi, lo, lo + 512), tag="rmm")
        for half in range(2):
            pi = 2 if half == 0 else 3
            k.op("vector", lambda e, pi=pi, half=half: e.tensor_scalar(
                out=tmpA_t[0:8, half * 1024:(half + 1) * 1024], in0=PS[pi][0:8, 0:1024],
                scalar1=CON["L0_ffn_br"][0][0:8, 0:1], scalar2=None, op0=ALU.add),
                reads=tuple(psb(pi)) + (CON["L0_ffn_br"][1],), writes=(tmpA_b,), tag="rbr")
        for tb in range(16):
            k.op("tensor", lambda e, tb=tb: e.transpose(
                PS[0][0:128, 0:8], tmpA_t[0:8, tb * 128:(tb + 1) * 128],
                CON["ident"][0][0:8, 0:8]),
                reads=(tmpA_b, CON["ident"][1]), writes=psb(0, 0, 8), tag="rtt")
            vCP(lgtm_t[:, tb * 8:(tb + 1) * 8], PS[0][:, 0:8], psb(0, 0, 8),
                (lgtm_b,), tag="rte")
        for tb in range(16):
            sl = slice(tb * 8, (tb + 1) * 8)
            k.op("vector", lambda e, sl=sl: e.tensor_reduce(
                out=tks_t[:, 0:1], in_=lgtm_t[:, sl], axis=AXX, op=ALU.max),
                reads=(lgtm_b,), writes=(tks_b,), tag="ramx")
            vTS(tmpB_t[:, 0:8], lgtm_t[:, sl], tks_t[:, 0:1], ALU.is_ge,
                (lgtm_b, tks_b), (tmpB_b,), tag="raeq")
            vSTT(tmpB_t[:, 8:16], tmpB_t[:, 0:8], -1000.0, CON["eidx8"][0][:, :],
                 ALU.mult, ALU.add, (tmpB_b, CON["eidx8"][1]), (tmpB_b,), tag="rav")
            k.op("vector", lambda e: e.tensor_reduce(
                out=tks_t[:, 1:2], in_=tmpB_t[:, 8:16], axis=AXX, op=ALU.min),
                reads=(tmpB_b,), writes=(tks_b,), tag="ramn")
            vTS(tmpB_t[:, 0:8], tmpB_t[:, 8:16], tks_t[:, 1:2], ALU.is_le,
                (tmpB_b, tks_b), (tmpB_b,), tag="ram1")
            k.op("tensor", lambda e, tb=tb: e.transpose(
                PS[1][0:8, 0:128], tmpB_t[:, 0:8], CON["ident"][0][:, :]),
                reads=(tmpB_b, CON["ident"][1]), writes=psb(1, 0, 128), tag="rmt")
            vCP(mskf_t[:, tb * 128:(tb + 1) * 128], PS[1][0:8, 0:128],
                psb(1, 0, 128), (mskf_b,), tag="rmte")
        dma("sync", tmpA_t[0:8, 0:1024], A["L0_ffn_b2"].ap()[:, :],
            reads=(HB["L0_ffn_b2"],), writes=(tmpA_b,), tag="b2l")
        vCP(b2e_t[:, :], tmpA_t[0:8, 0:1024], (tmpA_b,), (b2e_b,), tag="b2c")

        for c in range(NCH):
            k.op("vector", lambda e, c=c: e.memset(
                bbuf_t[:, c * TLOC:(c + 1) * TLOC], 0.0),
                writes=(bbuf_b,), tag="yz")

        for e8 in range(NEXP):
            dma("sync", stg_t[0:1, 0:2048], mskf_t[e8:e8 + 1, :],
                reads=(mskf_b,), writes=(stg_b[0],), tag="mrow")
            for s in range(4):
                k.op("tensor", lambda e, s=s: e.matmul(
                    PS[0][:, (s % 2) * 512:(s % 2) * 512 + 512], onesrow_bf[0:1, :],
                    stg_t[0:1, s * 512:(s + 1) * 512], start=True, stop=True),
                    reads=(stg_b[0], onesrow_bf_b),
                    writes=psb(0, (s % 2) * 512, (s % 2) * 512 + 512), tag="mbm")
                vCP(mskB_t[:, s * 512:(s + 1) * 512],
                    PS[0][:, (s % 2) * 512:(s % 2) * 512 + 512],
                    psb(0, (s % 2) * 512, (s % 2) * 512 + 512), (mskB_b,), tag="mbe")
            dma("sync", b1e_t[:, :], A["L0_ffn_b1_c"].ap()[e8, :, :],
                reads=(HB["L0_ffn_b1_c"],), writes=(b1e_b,), tag="b1l")
            for fq in range(4):
                for c in range(NCH):
                    dma("gpsimd", w1q_t[:, c * 1024:(c + 1) * 1024],
                        A["L0_ffn_W1T"].ap()[e8, c * 128:(c + 1) * 128,
                                             fq * 1024:(fq + 1) * 1024],
                        reads=(HB["L0_ffn_W1T"],), writes=(w1q_b,), tag="w1l")
                for fc in range(8):
                    dma("gpsimd", w2q_t[:, fc * 1024:(fc + 1) * 1024],
                        A["L0_ffn_W2T"].ap()[e8, fq * 1024 + fc * 128:
                                             fq * 1024 + (fc + 1) * 128, :],
                        reads=(HB["L0_ffn_W2T"],), writes=(w2q_b,), tag="w2l")
                for s in range(4):
                    for fc in range(8):
                        pi = fc % 2
                        lo = (fc // 2 % 2) * 512
                        mm1 = []
                        for c in range(NCH):
                            mm1.append(lambda e, c=c, fc=fc, s=s, pi=pi, lo=lo: e.matmul(
                                PS[pi][:, lo:lo + 512],
                                w1q_t[:, c * 1024 + fc * 128:c * 1024 + (fc + 1) * 128],
                                abuf_t[:, c * TLOC + s * 512:c * TLOC + (s + 1) * 512],
                                start=(c == 0), stop=(c == NCH - 1)))
                        k.op("tensor", mm1, reads=(w1q_b, abuf_b),
                             writes=psb(pi, lo, lo + 512), tag="h1m")
                        act(hq_t[:, fc * 512:(fc + 1) * 512], PS[pi][:, lo:lo + 512],
                            AF.Relu, tuple(psb(pi, lo, lo + 512)) + (b1e_b,), (hq_b,),
                            bias=b1e_t[:, fq * 8 + fc:fq * 8 + fc + 1], tag="h1r")
                        vTT(hq_t[:, fc * 512:(fc + 1) * 512],
                            hq_t[:, fc * 512:(fc + 1) * 512],
                            mskB_t[:, s * 512:(s + 1) * 512], ALU.mult,
                            (hq_b, mskB_b), (hq_b,), tag="h1k")
                    for oc in range(NCH):
                        pi = 2 + oc % 2
                        lo = (oc // 2 % 2) * 512
                        mm2 = []
                        for fc in range(8):
                            mm2.append(lambda e, fc=fc, oc=oc, pi=pi, lo=lo: e.matmul(
                                PS[pi][:, lo:lo + 512],
                                w2q_t[:, fc * 1024 + oc * 128:fc * 1024 + (oc + 1) * 128],
                                hq_t[:, fc * 512:(fc + 1) * 512],
                                start=(fc == 0), stop=False))
                        if e8 == NEXP - 1 and fq == 3:
                            mm2.append(lambda e, oc=oc, s=s, pi=pi, lo=lo: e.matmul(
                                PS[pi][:, lo:lo + 512],
                                b2e_t[:, oc * 128:(oc + 1) * 128],
                                mskf_t[:, s * 512:(s + 1) * 512],
                                start=False, stop=True))
                            k.op("tensor", mm2, reads=(w2q_b, hq_b, b2e_b, mskf_b),
                                 writes=psb(pi, lo, lo + 512), tag="h2m")
                        else:
                            k.op("tensor", mm2, reads=(w2q_b, hq_b),
                                 writes=psb(pi, lo, lo + 512), tag="h2m")
                        vTT(bbuf_t[:, oc * TLOC + s * 512:oc * TLOC + (s + 1) * 512],
                            bbuf_t[:, oc * TLOC + s * 512:oc * TLOC + (s + 1) * 512],
                            PS[pi][:, lo:lo + 512], ALU.add,
                            tuple(psb(pi, lo, lo + 512)) + (bbuf_b,), (bbuf_b,), tag="h2a")
        for oc in range(NCH):
            dma("sync", xc_t[:, :], A["x_hbm"].ap()[oc * 128:(oc + 1) * 128, :],
                reads=(HB["x_hbm"],), writes=(xc_b,), tag="m0rx")
            vTT(tmpD_t[:, :], bbuf_t[:, oc * TLOC:(oc + 1) * TLOC], xc_t[:, :],
                ALU.add, (bbuf_b, xc_b), (tmpD_b,), tag="m0ra")
            dma("sync", A["x_hbm"].ap()[oc * 128:(oc + 1) * 128, :], tmpD_t[:, :],
                reads=(tmpD_b,), writes=(HB["x_hbm"],), tag="m0rw")
        k.barrier()
        p4es.close()


    def projection_plain(w_ap, w_b, bias_t, bias_b, scale, func, dst, tag):
        for oc in range(NCH):
            for s in range(4):
                pi = oc % 2
                lo = (s % 2) * 512
                mm = []
                for c in range(NCH):
                    mm.append(lambda e, oc=oc, s=s, c=c, pi=pi, lo=lo: e.matmul(
                        PS[pi][:, lo:lo + 512],
                        w_ap[:, c * 1024 + oc * 128: c * 1024 + (oc + 1) * 128],
                        abuf_t[:, c * TLOC + s * 512: c * TLOC + (s + 1) * 512],
                        start=(c == 0), stop=(c == NCH - 1)))
                k.op("tensor", mm, reads=(w_b, abuf_b), writes=psb(pi, lo, lo + 512),
                     tag=tag + "mm")
                osl = stg_t[:, (oc % 2) * 2048 + s * 512:(oc % 2) * 2048 + (s + 1) * 512]
                if func is AF.Silu:
                    act(osl, PS[pi][:, lo:lo + 512], func,
                        tuple(psb(pi, lo, lo + 512)) + (bias_b,), (stg_b[oc % 2],),
                        bias=bias_t[:, oc:oc + 1], scale=scale, tag=tag + "ev")
                elif scale == 1.0:
                    k.op("vector", lambda e, osl=osl, pi=pi, lo=lo, oc=oc:
                         e.tensor_scalar(out=osl, in0=PS[pi][:, lo:lo + 512],
                                         scalar1=bias_t[:, oc:oc + 1], scalar2=None,
                                         op0=ALU.add),
                         reads=tuple(psb(pi, lo, lo + 512)) + (bias_b,),
                         writes=(stg_b[oc % 2],), tag=tag + "ev")
                else:
                    k.op("vector", lambda e, osl=osl, pi=pi, lo=lo, oc=oc:
                         e.tensor_scalar(out=osl, in0=PS[pi][:, lo:lo + 512],
                                         scalar1=bias_t[:, oc:oc + 1], scalar2=scale,
                                         op0=ALU.add, op1=ALU.mult),
                         reads=tuple(psb(pi, lo, lo + 512)) + (bias_b,),
                         writes=(stg_b[oc % 2],), tag=tag + "ev")
            if isinstance(dst, tuple):
                _, dt_t, dt_b = dst
                vCP(dt_t[:, oc * TLOC:(oc + 1) * TLOC],
                    stg_t[:, (oc % 2) * 2048:(oc % 2) * 2048 + 2048],
                    (stg_b[oc % 2],), (dt_b,), tag=tag + "cp")
            else:
                dma("sync", A[dst].ap()[oc * 128:(oc + 1) * 128, :],
                    stg_t[:, (oc % 2) * 2048:(oc % 2) * 2048 + 2048],
                    (stg_b[oc % 2],), (HB[dst],), tag=tag + "st")

    def load_w_bf(wname, dst_ap, dst_b, tag):
        for c in range(NCH):
            dma("gpsimd", dst_ap[:, c * 1024:(c + 1) * 1024],
                A[wname].ap()[c * 128:(c + 1) * 128, :],
                reads=(HB[wname],), writes=(dst_b,), tag=tag)

    if STAGE >= 5:
        ln_stats("x_hbm", "l1ln", want_comb=False)
        bcast_row(bc1_t, bc1_b, rowA_t, rowA_b, "l1a")
        bcast_row(bc2_t, bc2_b, rowB_t, rowB_b, "l1b")
        xf_build("x_hbm", "l1xf")
        k.op("vector", lambda e: e.memset(tks_t[:, 8:16], 0.0), writes=(tks_b,),
             tag="l1xmz")
        for c in range(NCH):
            k.op("vector", lambda e, c=c: e.tensor_reduce(
                out=tks_t[:, c:c + 1], in_=abuf_t[:, c * TLOC:(c + 1) * TLOC],
                axis=AXX, op=ALU.add),
                reads=(abuf_b,), writes=(tks_b,), tag="l1xm")
        dma("sync", A["agst_in"].ap()[:, :], tks_t[:, 0:16],
            reads=(tks_b,), writes=(HB["agst_in"],), tag="l1xms")
        allgather("agst", "AGst")

        p5es = ExitStack()
        w5_t = p5es.enter_context(nc.sbuf_tensor("w5", [128, 8192], BF16))
        w5_b = Buf("w5")
        rc5_t = p5es.enter_context(nc.sbuf_tensor("rc5", [128, 2048], F32))
        rc5_b = Buf("rc5")
        rs5_t = p5es.enter_context(nc.sbuf_tensor("rs5", [128, 2048], F32))
        rs5_b = Buf("rs5")
        bia_t = p5es.enter_context(nc.sbuf_tensor("bia", [128, 8], F32))
        bia_b = Buf("bia")
        dma("sync", rc5_t[:, :], A["ropeC"].ap()[:, :], reads=(HB["ropeC"],),
            writes=(rc5_b,), tag="rcl")
        dma("sync", rs5_t[:, :], A["ropeS"].ap()[:, :], reads=(HB["ropeS"],),
            writes=(rs5_b,), tag="rsl")

        def rope_inplace(tag):
            if os.environ.get("MK_NOROPE", "") == "1":
                return
            for c in range(NCH):
                for s in range(4):
                    sl = slice(c * TLOC + s * 512, c * TLOC + (s + 1) * 512)
                    ts_ = slice(s * 512, (s + 1) * 512)
                    pi = s % 2
                    k.op("tensor", lambda e, sl=sl, pi=pi: e.matmul(
                        PS[pi][:, 0:512], CB["permT"][0][:, :], bbuf_t[:, sl],
                        start=True, stop=True),
                        reads=(CB["permT"][1], bbuf_b), writes=psb(pi, 0, 512),
                        tag=tag + "rot")
                    vTT(tmpA_t[:, 0:512], bbuf_t[:, sl], rc5_t[:, ts_], ALU.mult,
                        (bbuf_b, rc5_b), (tmpA_b,), tag=tag + "c")
                    vTT(tmpB_t[:, 0:512], PS[pi][:, 0:512], rs5_t[:, ts_], ALU.mult,
                        tuple(psb(pi, 0, 512)) + (rs5_b,), (tmpB_b,), tag=tag + "s")
                    vTT(bbuf_t[:, sl], tmpA_t[:, 0:512], tmpB_t[:, 0:512], ALU.add,
                        (tmpA_b, tmpB_b), (bbuf_b,), tag=tag + "a")

        load_w_bf("L1_attn_wq_T", w5_t[:, :], w5_b, "l1wq")
        projection_plain(w5_t[:, :], w5_b, CON["L1_attn_bq_c"][0],
                         CON["L1_attn_bq_c"][1], 0.125, AF.Copy,
                         ("sbuf", bbuf_t, bbuf_b), "l1pq")
        rope_inplace("l1rq")
        for oc in range(NCH):
            dma("sync", A["agq_in"].ap()[oc * 128:(oc + 1) * 128, :],
                bbuf_t[:, oc * TLOC:(oc + 1) * TLOC], reads=(bbuf_b,),
                writes=(HB["agq_in"],), tag="l1qs")
        allgather("agq", "AGq1")
        load_w_bf("L1_attn_wk_T", w5_t[:, :], w5_b, "l1wk")
        projection_plain(w5_t[:, :], w5_b, CON["L1_attn_bk_c"][0],
                         CON["L1_attn_bk_c"][1], 1.0, AF.Copy,
                         ("sbuf", bbuf_t, bbuf_b), "l1pk")
        rope_inplace("l1rk")
        for oc in range(NCH):
            dma("sync", A["agk_in"].ap()[oc * 128:(oc + 1) * 128, :],
                bbuf_t[:, oc * TLOC:(oc + 1) * TLOC], reads=(bbuf_b,),
                writes=(HB["agk_in"],), tag="l1ks")
        allgather("agk", "AGk1")
        load_w_bf("L1_attn_wv_T", w5_t[:, :], w5_b, "l1wv")
        projection_plain(w5_t[:, :], w5_b, CON["L1_attn_bv_c"][0],
                         CON["L1_attn_bv_c"][1], 1.0, AF.Copy, "agv_in", "l1pv")
        allgather("agv", "AGv1")
        load_w_bf("L1_attn_wg_T", w5_t[:, :], w5_b, "l1wg")
        projection_plain(w5_t[:, :], w5_b, CON["L1_attn_bg_c"][0],
                         CON["L1_attn_bg_c"][1], 1.0, AF.Silu, "gate_hbm", "l1pg")
        k.barrier()
        p5es.close()

    if STAGE >= 6:
        attn_phase(1, "a1")

    if STAGE >= 7:
        onorm_phase("o1", with_sq_stats=False)
        p7es = ExitStack()
        w7_t = p7es.enter_context(nc.sbuf_tensor("w7", [128, 8192], BF16))
        w7_b = Buf("w7")
        gsr_t = p7es.enter_context(nc.sbuf_tensor("gsr", [1, 64], F32))
        gsr_b = Buf("gsr")
        for c in range(NCH):
            dma("sync", stg_t[:, 0:2048], A["onorm_hbm"].ap()[c * 128:(c + 1) * 128, :],
                reads=(HB["onorm_hbm"],), writes=(stg_b[0],), tag="g7l")
            k.op("vector", lambda e: e.tensor_reduce(
                out=tks_t[:, 0:1], in_=stg_t[:, 0:2048], axis=AXX, op=ALU.add),
                reads=(stg_b[0],), writes=(tks_b,), tag="g7s")
            vTT(tmpB_t[:, :], stg_t[:, 0:2048], stg_t[:, 0:2048], ALU.mult,
                (stg_b[0],), (tmpB_b,), tag="g7q")
            k.op("vector", lambda e: e.tensor_reduce(
                out=tks_t[:, 1:2], in_=tmpB_t[:, :], axis=AXX, op=ALU.add),
                reads=(tmpB_b,), writes=(tks_b,), tag="g7r")
            k.op("tensor", lambda e: e.matmul(
                PS[3][0:1, 0:2], tks_t[:, 0:1], CON["sel2T"][0][:, :],
                start=True, stop=True),
                reads=(CON["sel2T"][1], tks_b), writes=psb(3, 0, 2), tag="g7m1")
            k.op("tensor", lambda e: e.matmul(
                PS[3][0:1, 4:6], tks_t[:, 1:2], CON["sel2T"][0][:, :],
                start=True, stop=True),
                reads=(CON["sel2T"][1], tks_b), writes=psb(3, 4, 6), tag="g7m2")
            vCP(gsr_t[0:1, 2 * c:2 * c + 2], PS[3][0:1, 0:2], psb(3, 0, 2),
                (gsr_b,), tag="g7e1")
            vCP(gsr_t[0:1, 16 + 2 * c:18 + 2 * c], PS[3][0:1, 4:6], psb(3, 4, 6),
                (gsr_b,), tag="g7e2")
        dma("sync", A["agg_in"].ap()[:, :], gsr_t[0:1, 0:32], reads=(gsr_b,),
            writes=(HB["agg_in"],), tag="g7w")
        allgather("agg", "AGg")
        dma("sync", tmpA_t[0:1, 0:32], A["agg_out"].ap()[0:1, :], reads=(HB["agg_out"],),
            writes=(tmpA_b,), tag="g7r1")
        dma("sync", tmpA_t[0:1, 32:64], A["agg_out"].ap()[1:2, :], reads=(HB["agg_out"],),
            writes=(tmpA_b,), tag="g7r2")
        vTT(gsr_t[0:1, 0:32], tmpA_t[0:1, 0:32], tmpA_t[0:1, 32:64], ALU.add,
            (tmpA_b,), (gsr_b,), tag="g7t")
        NTD = float(T * HD)
        vTS(gsr_t[0:1, 32:48], gsr_t[0:1, 0:16], 1.0 / NTD, ALU.mult, (gsr_b,),
            (gsr_b,), tag="g7mu")
        vTT(tmpA_t[0:1, 0:16], gsr_t[0:1, 32:48], gsr_t[0:1, 32:48], ALU.mult,
            (gsr_b,), (tmpA_b,), tag="g7m2b")
        vSTT(gsr_t[0:1, 48:64], gsr_t[0:1, 16:32], 1.0 / NTD, tmpA_t[0:1, 0:16],
             ALU.mult, ALU.subtract, (gsr_b, tmpA_b), (gsr_b,), tag="g7v")
        act(tmpA_t[0:1, 0:16], gsr_t[0:1, 48:64], AF.Ln, (gsr_b,), (tmpA_b,),
            bias=EPS_LN, tag="g7ln")
        act(gsr_t[0:1, 48:64], tmpA_t[0:1, 0:16], AF.Exp, (tmpA_b,), (gsr_b,),
            scale=-0.5, tag="g7rs")
        for c in range(NCH):
            vCP(tmpA_t[0:1, 0:2], gsr_t[0:1, 32 + 2 * c:34 + 2 * c], (gsr_b,),
                (tmpA_b,), tag="g7c1")
            vCP(tmpA_t[0:1, 2:4], gsr_t[0:1, 48 + 2 * c:50 + 2 * c], (gsr_b,),
                (tmpA_b,), tag="g7c2")
            k.op("tensor", lambda e: e.transpose(
                PS[2][0:2, 0:1], tmpA_t[0:1, 0:2], CON["ident"][0][0:1, 0:1]),
                reads=(tmpA_b, CON["ident"][1]), writes=psb(2, 0, 1), tag="g7t1")
            k.op("tensor", lambda e: e.transpose(
                PS[2][0:2, 4:5], tmpA_t[0:1, 2:4], CON["ident"][0][0:1, 0:1]),
                reads=(tmpA_b, CON["ident"][1]), writes=psb(2, 4, 5), tag="g7t2")
            vCP(tmpB_t[0:2, 0:1], PS[2][0:2, 0:1], psb(2, 0, 1), (tmpB_b,), tag="g7te1")
            vCP(tmpB_t[0:2, 1:2], PS[2][0:2, 4:5], psb(2, 4, 5), (tmpB_b,), tag="g7te2")
            k.op("tensor", lambda e: e.matmul(
                PS[3][:, 0:1], CON["sel2"][0][:, :], tmpB_t[0:2, 0:1],
                start=True, stop=True),
                reads=(CON["sel2"][1], tmpB_b), writes=psb(3, 0, 1), tag="g7mm1")
            vCP(sm_t[:, 46:47], PS[3][:, 0:1], psb(3, 0, 1), (sm_b,), tag="g7me1")
            k.op("tensor", lambda e: e.matmul(
                PS[3][:, 0:1], CON["sel2"][0][:, :], tmpB_t[0:2, 1:2],
                start=True, stop=True),
                reads=(CON["sel2"][1], tmpB_b), writes=psb(3, 0, 1), tag="g7mm2")
            vCP(sm_t[:, 47:48], PS[3][:, 0:1], psb(3, 0, 1), (sm_b,), tag="g7me2")
            dma("sync", stg_t[:, 0:2048], A["onorm_hbm"].ap()[c * 128:(c + 1) * 128, :],
                reads=(HB["onorm_hbm"],), writes=(stg_b[0],), tag="g7ol")
            dma("sync", stg_t[:, 2048:4096], A["gate_hbm"].ap()[c * 128:(c + 1) * 128, :],
                reads=(HB["gate_hbm"],), writes=(stg_b[1],), tag="g7gl")
            k.op("vector", lambda e, c=c: e.tensor_scalar(
                out=tmpA_t[:, :], in0=stg_t[:, 0:2048], scalar1=sm_t[:, 46:47],
                scalar2=sm_t[:, 47:48], op0=ALU.subtract, op1=ALU.mult),
                reads=(stg_b[0], sm_b), writes=(tmpA_b,), tag="g7n1")
            k.op("vector", lambda e, c=c: e.tensor_scalar(
                out=tmpB_t[:, :], in0=tmpA_t[:, :],
                scalar1=CON["L1_attn_gamma_c"][0][:, c:c + 1],
                scalar2=CON["L1_attn_beta_c"][0][:, c:c + 1],
                op0=ALU.mult, op1=ALU.add),
                reads=(tmpA_b, CON["L1_attn_gamma_c"][1], CON["L1_attn_beta_c"][1]),
                writes=(tmpB_b,), tag="g7n2")
            vTT(abuf_t[:, c * TLOC:(c + 1) * TLOC], tmpB_t[:, :], stg_t[:, 2048:4096],
                ALU.mult, (tmpB_b, stg_b[1]), (abuf_b,), tag="g7n3")
        load_w_bf("L1_attn_wo_T", w7_t[:, :], w7_b, "l1wo")
        for oc in range(NCH):
            for s in range(4):
                pi = oc % 2
                lo = (s % 2) * 512
                mm = []
                for c in range(NCH):
                    mm.append(lambda e, oc=oc, s=s, c=c, pi=pi, lo=lo: e.matmul(
                        PS[pi][:, lo:lo + 512],
                        w7_t[:, c * 1024 + oc * 128: c * 1024 + (oc + 1) * 128],
                        abuf_t[:, c * TLOC + s * 512: c * TLOC + (s + 1) * 512],
                        start=(c == 0), stop=(c == NCH - 1)))
                k.op("tensor", mm, reads=(w7_b, abuf_b), writes=psb(pi, lo, lo + 512),
                     tag="l1po")
                k.op("vector", lambda e, pi=pi, lo=lo, oc=oc, s=s: e.tensor_scalar(
                    out=stg_t[:, (oc % 2) * 2048 + s * 512:(oc % 2) * 2048 + (s + 1) * 512],
                    in0=PS[pi][:, lo:lo + 512],
                    scalar1=CON["L1_attn_bo_c"][0][:, oc:oc + 1], scalar2=None,
                    op0=ALU.add),
                    reads=tuple(psb(pi, lo, lo + 512)) + (CON["L1_attn_bo_c"][1],),
                    writes=(stg_b[oc % 2],), tag="l1poe")
            dma("sync", xc_t[:, :], A["x_hbm"].ap()[oc * 128:(oc + 1) * 128, :],
                reads=(HB["x_hbm"],), writes=(xc_b,), tag="l1rx")
            vTT(tmpD_t[:, :], stg_t[:, (oc % 2) * 2048:(oc % 2) * 2048 + 2048],
                xc_t[:, :], ALU.add, (stg_b[oc % 2], xc_b), (tmpD_b,), tag="l1ra")
            dma("sync", A["x_hbm"].ap()[oc * 128:(oc + 1) * 128, :], tmpD_t[:, :],
                reads=(tmpD_b,), writes=(HB["x_hbm"],), tag="l1rw")
        k.barrier()
        p7es.close()

    if STAGE >= 8:
        ln_stats("x_hbm", "m1ln", want_comb=False)
        bcast_row(bc1_t, bc1_b, rowA_t, rowA_b, "m1a")
        bcast_row(bc2_t, bc2_b, rowB_t, rowB_b, "m1b")
        xf_build("x_hbm", "m1xf")
        p8es = ExitStack()
        w8_t = p8es.enter_context(nc.sbuf_tensor("w8", [128, 8192], BF16))
        w8_b = Buf("w8")
        lg8_t = p8es.enter_context(nc.sbuf_tensor("lg8", [128, 128], F32))
        lg8_b = Buf("lg8")
        eb8_t = p8es.enter_context(nc.sbuf_tensor("eb8", [128, 8], F32))
        eb8_b = Buf("eb8")
        for c in range(NCH):
            dma("gpsimd", w8_t[:, 0:8], A["L1_ffn_gwT"].ap()[c * 128:(c + 1) * 128, :],
                reads=(HB["L1_ffn_gwT"],), writes=(w8_b,), tag="c8wl")
            for s in range(4):
                pi = 2 if s < 2 else 3
                lo = (s % 2) * 512
                k.op("tensor", lambda e, c=c, s=s, pi=pi, lo=lo: e.matmul(
                    PS[pi][0:8, lo:lo + 512], w8_t[:, 0:8],
                    abuf_t[:, c * TLOC + s * 512:c * TLOC + (s + 1) * 512],
                    start=(c == 0), stop=(c == NCH - 1)),
                    reads=(w8_b, abuf_b), writes=psb(pi, lo, lo + 512), tag="c8mm")
        for half in range(2):
            pi = 2 if half == 0 else 3
            k.op("vector", lambda e, pi=pi, half=half: e.tensor_scalar(
                out=tmpA_t[0:8, half * 1024:(half + 1) * 1024], in0=PS[pi][0:8, 0:1024],
                scalar1=CON["L1_ffn_gb"][0][0:8, 0:1], scalar2=None, op0=ALU.add),
                reads=tuple(psb(pi)) + (CON["L1_ffn_gb"][1],), writes=(tmpA_b,), tag="c8br")
        first_cnt = [True]
        for tb in range(16):
            k.op("tensor", lambda e, tb=tb: e.transpose(
                PS[0][0:128, 0:8], tmpA_t[0:8, tb * 128:(tb + 1) * 128],
                CON["ident"][0][0:8, 0:8]),
                reads=(tmpA_b, CON["ident"][1]), writes=psb(0, 0, 8), tag="c8tt")
            vCP(lg8_t[:, 0:8], PS[0][:, 0:8], psb(0, 0, 8), (lg8_b,), tag="c8te")
            k.op("vector", lambda e: e.tensor_reduce(
                out=tks_t[:, 0:1], in_=lg8_t[:, 0:8], axis=AXX, op=ALU.max),
                reads=(lg8_b,), writes=(tks_b,), tag="c8mx")
            vTS(tmpB_t[:, 0:8], lg8_t[:, 0:8], tks_t[:, 0:1], ALU.is_ge,
                (lg8_b, tks_b), (tmpB_b,), tag="c8eq")
            vSTT(tmpB_t[:, 8:16], tmpB_t[:, 0:8], -1000.0, CON["eidx8"][0][:, :],
                 ALU.mult, ALU.add, (tmpB_b, CON["eidx8"][1]), (tmpB_b,), tag="c8v")
            k.op("vector", lambda e: e.tensor_reduce(
                out=tks_t[:, 1:2], in_=tmpB_t[:, 8:16], axis=AXX, op=ALU.min),
                reads=(tmpB_b,), writes=(tks_b,), tag="c8mn")
            vTS(tmpB_t[:, 0:8], tmpB_t[:, 8:16], tks_t[:, 1:2], ALU.is_le,
                (tmpB_b, tks_b), (tmpB_b,), tag="c8m1")
            vSTT(lg8_t[:, 8:16], tmpB_t[:, 0:8], -1.0e9, lg8_t[:, 0:8],
                 ALU.mult, ALU.add, (tmpB_b, lg8_b), (lg8_b,), tag="c8l2")
            k.op("vector", lambda e: e.tensor_reduce(
                out=tks_t[:, 2:3], in_=lg8_t[:, 8:16], axis=AXX, op=ALU.max),
                reads=(lg8_b,), writes=(tks_b,), tag="c8mx2")
            vTS(tmpB_t[:, 16:24], lg8_t[:, 8:16], tks_t[:, 2:3], ALU.is_ge,
                (lg8_b, tks_b), (tmpB_b,), tag="c8eq2")
            vSTT(tmpB_t[:, 24:32], tmpB_t[:, 16:24], -1000.0, CON["eidx8"][0][:, :],
                 ALU.mult, ALU.add, (tmpB_b, CON["eidx8"][1]), (tmpB_b,), tag="c8v2")
            k.op("vector", lambda e: e.tensor_reduce(
                out=tks_t[:, 3:4], in_=tmpB_t[:, 24:32], axis=AXX, op=ALU.min),
                reads=(tmpB_b,), writes=(tks_b,), tag="c8mn2")
            vTS(tmpB_t[:, 16:24], tmpB_t[:, 24:32], tks_t[:, 3:4], ALU.is_le,
                (tmpB_b, tks_b), (tmpB_b,), tag="c8m2")
            vTT(tmpB_t[:, 0:8], tmpB_t[:, 0:8], tmpB_t[:, 16:24], ALU.add,
                (tmpB_b,), (tmpB_b,), tag="c8ms")
            k.op("tensor", lambda e, tb=tb, st=first_cnt[0]: e.matmul(
                PS[1][0:1, 0:8], CON["ones_col"][0][:, 0:1], tmpB_t[:, 0:8],
                start=st, stop=(tb == 15)),
                reads=(tmpB_b, CON["ones_col"][1]), writes=psb(1, 0, 8), tag="c8cm")
            first_cnt[0] = False
        vCP(tmpA_t[0:1, 0:8], PS[1][0:1, 0:8], psb(1, 0, 8), (tmpA_b,), tag="c8ce")
        dma("sync", A["agc_in"].ap()[:, :], tmpA_t[0:1, 0:8], reads=(tmpA_b,),
            writes=(HB["agc_in"],), tag="c8cw")
        k.op("gpsimd", lambda e: e.collective_compute(
            "AllGather", ALU.bypass, replica_groups=[[0, 1, 2, 3, 4, 5, 6, 7]],
            ins=[A["agc_in"].ap().opt()], outs=[A["agc_out"].ap().opt()]),
            reads=(HB["agc_in"],), writes=(HB["agc_out"],), cc=True, tag="AGc")
        dma("sync", tmpA_t[0:8, 8:16], A["agc_out"].ap()[:, :], reads=(HB["agc_out"],),
            writes=(tmpA_b,), tag="c8cr")
        k.op("tensor", lambda e: e.matmul(
            PS[1][0:1, 0:8], CON["ones_col"][0][0:8, 0:1], tmpA_t[0:8, 8:16],
            start=True, stop=True),
            reads=(tmpA_b, CON["ones_col"][1]), writes=psb(1, 0, 8), tag="c8tm")
        vTS(tmpA_t[0:1, 8:16], PS[1][0:1, 0:8], 1.0 / (2.0 * BATCH * T), ALU.mult,
            psb(1, 0, 8), (tmpA_b,), tag="c8cn")
        for c in range(NCH):
            k.op("vector", lambda e, c=c: e.memset(
                bbuf_t[:, c * TLOC:(c + 1) * TLOC], 0.0),
                writes=(bbuf_b,), tag="y8z")
        for e8 in range(NEXP):
            for c in range(NCH):
                dma("gpsimd", w8_t[:, c * 1024:(c + 1) * 1024],
                    A["L1_ffn_ewT"].ap()[e8, c * 128:(c + 1) * 128, :],
                    reads=(HB["L1_ffn_ewT"],), writes=(w8_b,), tag="e8wl")
            dma("sync", eb8_t[:, :], A["L1_ffn_eb_c"].ap()[e8, :, :],
                reads=(HB["L1_ffn_eb_c"],), writes=(eb8_b,), tag="e8bl")
            k.op("tensor", lambda e, e8=e8: e.matmul(
                PS[0][:, 0:1], ones_row_f32[0:1, :], tmpA_t[0:1, 8 + e8:9 + e8],
                start=True, stop=True),
                reads=(tmpA_b, ones_row_f32_b), writes=psb(0, 0, 1), tag="e8cb")
            vCP(sm_t[:, 48:49], PS[0][:, 0:1], psb(0, 0, 1), (sm_b,), tag="e8ce")
            vTS(eb8_t[:, :], eb8_t[:, :], sm_t[:, 48:49], ALU.mult,
                (eb8_b, sm_b), (eb8_b,), tag="e8bs")
            for oc in range(NCH):
                for s in range(4):
                    pi = oc % 2
                    lo = (s % 2) * 512
                    mm = []
                    for c in range(NCH):
                        mm.append(lambda e, oc=oc, s=s, c=c, pi=pi, lo=lo: e.matmul(
                            PS[pi][:, lo:lo + 512],
                            w8_t[:, c * 1024 + oc * 128: c * 1024 + (oc + 1) * 128],
                            abuf_t[:, c * TLOC + s * 512: c * TLOC + (s + 1) * 512],
                            start=(c == 0), stop=(c == NCH - 1)))
                    k.op("tensor", mm, reads=(w8_b, abuf_b), writes=psb(pi, lo, lo + 512),
                         tag="e8mm")
                    act(stg_t[:, (oc % 2) * 2048 + s * 512:(oc % 2) * 2048 + (s + 1) * 512],
                        PS[pi][:, lo:lo + 512], AF.Relu,
                        tuple(psb(pi, lo, lo + 512)) + (eb8_b, sm_b), (stg_b[oc % 2],),
                        bias=eb8_t[:, oc:oc + 1], scale=sm_t[:, 48:49], tag="e8ev")
                vTT(bbuf_t[:, oc * TLOC:(oc + 1) * TLOC],
                    bbuf_t[:, oc * TLOC:(oc + 1) * TLOC],
                    stg_t[:, (oc % 2) * 2048:(oc % 2) * 2048 + 2048], ALU.add,
                    (stg_b[oc % 2], bbuf_b), (bbuf_b,), tag="e8acc")
        for oc in range(NCH):
            dma("sync", xc_t[:, :], A["x_hbm"].ap()[oc * 128:(oc + 1) * 128, :],
                reads=(HB["x_hbm"],), writes=(xc_b,), tag="m1rx")
            vTT(tmpD_t[:, :], bbuf_t[:, oc * TLOC:(oc + 1) * TLOC], xc_t[:, :],
                ALU.add, (bbuf_b, xc_b), (tmpD_b,), tag="m1ra")
            dma("sync", A["out"].ap()[oc * 128:(oc + 1) * 128, :], tmpD_t[:, :],
                reads=(tmpD_b,), writes=(HB["out"],), tag="m1rw")
        p8es.close()

    if STAGE < 3:
        for c in range(NCH):
            dma("sync", A["out"].ap()[c * 128:(c + 1) * 128, :],
                A["x_in"].ap()[c * 128:(c + 1) * 128, :],
                (HB["x_in"], HB["agv_out"], HB["agq_out"], HB["agk_out"]),
                (HB["out"],), tag="outph")
    elif STAGE < 8:
        for c in range(NCH):
            dma("sync", A["out"].ap()[c * 128:(c + 1) * 128, :],
                A["x_hbm"].ap()[c * 128:(c + 1) * 128, :],
                (HB["x_hbm"],), (HB["out"],), tag="outph")
    if DEBUG:
        for (nm, shp, dt) in dbg_list:
            A["dbg_" + nm] = nc.dram_tensor("dbg_" + nm, shp, dt, kind="ExternalOutput")
            HB["dbg_" + nm] = Buf("hb_dbg_" + nm)
            dma("sync", A["dbg_" + nm].ap()[:, :], A[nm].ap()[:, :],
                reads=(HB[nm],), writes=(HB["dbg_" + nm],), tag="dbgcp")
        k.op("sync", lambda e: e.nop(),
             reads=tuple(HB["dbg_" + nm] for (nm, _, _) in dbg_list), tag="findbg")
    k.op("sync", lambda e: e.nop(), reads=(HB["out"], HB["agq_out"], HB["agk_out"],
                                           HB["agv_out"]), tag="fin")

    blk = es.enter_context(nc.Block())
    k.finalize(blk, sems)
    return nc


def kernel(**inputs):
    ins = [_host_inputs(inputs, c) for c in range(NCORE)]
    nc = build_nc()
    res = run_bass_kernel_spmd(nc, ins, core_ids=list(range(NCORE)))
    out = np.empty((BATCH, T, D), np.float32)
    for c in range(NCORE):
        b, par = c // 2, c % 2
        out[b, par * TLOC:(par + 1) * TLOC, :] = np.asarray(res.results[c]["out"]).T
    return out


def kernel_debug(**inputs):
    ins = [_host_inputs(inputs, c) for c in range(NCORE)]
    nc = build_nc()
    return run_bass_kernel_spmd(nc, ins, core_ids=list(range(NCORE)))
